# version 08
# baseline (speedup 1.0000x reference)
import numpy as np
import concourse.bass as bass
import concourse.mybir as mybir
from concourse.bass_utils import run_bass_kernel_spmd
from contextlib import ExitStack

F32 = mybir.dt.float32
BF16 = mybir.dt.bfloat16
I32 = mybir.dt.int32
AF = mybir.ActivationFunctionType
ALU = mybir.AluOpType

L = 2048
D = 2048
G = 16
NST = 32
RK = 64
TB = 16
NCORE = 8
NBQ = 4
NBS = 8
CQ = 4
TQ = 512
IBLK = 11
FI = 1408
ELOC = 2

_DBG = {"skip_y": False, "skip_ffn": False, "tap_b8": False,
        "idle_ffn": False, "tap2": False}


class Plan:
    ENGINES = ("sync", "gpsimd", "vector", "scalar", "tensor")

    def __init__(self):
        self.steps = {e: [] for e in self.ENGINES}
        self.cnt = {c: 0 for c in (
            "d_cst", "d_x", "d_opre", "d_tk", "d_wc", "d_dr", "d_bc",
            "d_wg", "d_wu", "d_wd", "d_x0q", "d_y", "d_y1", "d_y2", "d_y3", "d_f", "d_fin", "d_tp",
            "sv", "ss", "st", "sg", "cc")}

    def add(self, eng, emit, waits=(), inc=None, amt=None):
        tok = None
        if inc is not None:
            if amt is None:
                amt = 16 if inc.startswith("d_") else 1
            self.cnt[inc] += amt
            tok = (inc, self.cnt[inc])
        self.steps[eng].append(
            (emit, [w for w in waits if w is not None], inc, amt))
        return tok

    def barrier(self):
        snap = [(c, v) for c, v in self.cnt.items() if v > 0]
        for e in self.ENGINES:
            self.steps[e].append((None, snap, None, None))

    def play(self, eng_name, engine, sems):
        for emit, waits, inc, amt in self.steps[eng_name]:
            for ch, val in waits:
                engine.wait_ge(sems[ch], val)
            inst = emit(engine) if emit is not None else None
            if inc is not None:
                inst.then_inc(sems[inc], amt)


def build():
    nc = bass.Bass("TRN2", target_bir_lowering=False)

    xb_d = nc.dram_tensor("xb", [L, D], F32, kind="ExternalInput")
    x0_d = nc.dram_tensor("x0", [L, D], F32, kind="ExternalInput")
    wbcT_d = nc.dram_tensor("wbcT", [D, 128], F32, kind="ExternalInput")
    wdtT_d = nc.dram_tensor("wdtT", [RK, D], F32, kind="ExternalInput")
    bdt_d = nc.dram_tensor("bdt", [D], F32, kind="ExternalInput")
    alog_d = nc.dram_tensor("alog", [D, NST], F32, kind="ExternalInput")
    dp_d = nc.dram_tensor("dp", [D], F32, kind="ExternalInput")
    tki_d = nc.dram_tensor("tki", [L, 4], I32, kind="ExternalInput")
    tkw_d = nc.dram_tensor("tkw", [L, 4], F32, kind="ExternalInput")
    wg_d = nc.dram_tensor("wg", [ELOC, D, FI], F32, kind="ExternalInput")
    wu_d = nc.dram_tensor("wu", [ELOC, D, FI], F32, kind="ExternalInput")
    wd_d = nc.dram_tensor("wd", [ELOC, FI, D], F32, kind="ExternalInput")
    ecol_d = nc.dram_tensor("ecol", [128, ELOC], F32, kind="ExternalInput")
    m0_d = nc.dram_tensor("m0", [128, 1], F32, kind="ExternalInput")
    out_d = nc.dram_tensor("out", [L, D], F32, kind="ExternalOutput")
    ffn_in_d = nc.dram_tensor("ffn_in", [D, L], F32, kind="Internal")
    wgb_d = nc.dram_tensor("wgb", [ELOC, D, FI], BF16, kind="Internal")
    y_d = nc.dram_tensor("ystg", [CQ, NBQ, G, 128, TQ], F32, kind="Internal")
    wub_d = nc.dram_tensor("wub", [ELOC, D, FI], BF16, kind="Internal")
    wdb_d = nc.dram_tensor("wdb", [ELOC, FI, D], BF16, kind="Internal")
    ffn_out_d = nc.dram_tensor("ffn_out", [D, L], F32, kind="Internal")
    dbg_d = nc.dram_tensor("dbg", [L, D], F32, kind="Internal")
    if _DBG["tap2"]:
        tapp_d = nc.dram_tensor("tapp", [7, 128, NBS, TQ], F32,
                                kind="ExternalOutput")
    if _DBG["tap_b8"]:
        tapb_d = nc.dram_tensor("tapb", [16, 128, NBS, TQ], F32,
                                kind="ExternalOutput")
        tapa_d = nc.dram_tensor("tapa", [16, 128, NBS, TQ], F32,
                                kind="ExternalOutput")
        tapy_d = nc.dram_tensor("tapy", [CQ, NBQ, G, 128, TQ], F32,
                                kind="ExternalOutput")
        tapp_d = nc.dram_tensor("tapp", [5, 128, NBS, TQ], F32,
                                kind="ExternalOutput")
    y_out_d = dbg_d if _DBG["skip_y"] else out_d
    f_out_d = dbg_d if _DBG["skip_ffn"] else out_d

    p = Plan()

    with ExitStack() as top:
        E = top.enter_context
        A_sb = E(nc.sbuf_tensor("A_sb", [128, G, NST], F32))
        bdt_sb = E(nc.sbuf_tensor("bdt_sb", [128, G], F32))
        wdtT_sb = E(nc.sbuf_tensor("wdtT_sb", [RK, G, 128], BF16))
        eye_sb = E(nc.sbuf_tensor("eye_sb", [128, 128], BF16))
        dtr_sb = E(nc.sbuf_tensor("dtr_sb", [RK, L], BF16))
        B_sb = E(nc.sbuf_tensor("B_sb", [NST, L], BF16))
        C_sb = E(nc.sbuf_tensor("C_sb", [NST, L], BF16))
        du_sb = E(nc.sbuf_tensor("du_sb", [128, G, L], BF16))
        hcol_sb = E(nc.sbuf_tensor("hcol_sb", [128, G, NST], F32))
        stat0r_sb = E(nc.sbuf_tensor("stat0r_sb", [128, TB], F32))
        stat0n_sb = E(nc.sbuf_tensor("stat0n_sb", [128, TB], F32))
        rrep_sb = E(nc.sbuf_tensor("rrep_sb", [128, ELOC, L], BF16))
        m0_sb = E(nc.sbuf_tensor("m0_sb", [128, 1], F32))
        ecol_sb = E(nc.sbuf_tensor("ecol_sb", [128, ELOC], F32))

        ps_y = E(nc.psum_tensor("ps_y", [128, TQ], F32))
        ps_a = E(nc.psum_tensor("ps_a", [128, 512], F32))
        ps_b = E(nc.psum_tensor("ps_b", [128, 512], F32))
        ps_g = E(nc.psum_tensor("ps_g", [128, 512], F32))
        ps_u = E(nc.psum_tensor("ps_u", [128, 512], F32))
        ps_d = E(nc.psum_tensor("ps_d", [128, 512], F32))
        ps_t = E(nc.psum_tensor("ps_t", [128, 512], F32))
        ps_x = E(nc.psum_tensor("ps_x", [128, 512], F32))
        PSAB = [ps_a, ps_b]

        sems = {ch: E(nc.semaphore(ch)) for ch in p.cnt}

        t_al = p.add("sync", lambda s: s.dma_start(
            A_sb[:, :, :],
            bass.AP(alog_d, 0, [[NST, 128], [128 * NST, G], [1, NST]])),
            inc="d_cst")
        t_bd = p.add("sync", lambda s: s.dma_start(
            bdt_sb[:, :], bass.AP(bdt_d, 0, [[1, 128], [128, G]]),
            allow_slow_non_contiguous=True), inc="d_cst")
        p.add("sync", lambda s: s.dma_start(m0_sb[:, :], m0_d.ap()),
              inc="d_cst")
        t_ec = p.add("sync", lambda s: s.dma_start(
            ecol_sb[:, :], ecol_d.ap()), inc="d_cst")
        tok_cst = ("d_cst", p.cnt["d_cst"])
        t_al = t_bd = t_ec = tok_cst
        t_wdt = p.add("gpsimd", lambda g: g.dma_start(
            wdtT_sb[:, :, :],
            bass.AP(wdtT_d, 0, [[D, RK], [128, G], [1, 128]])), inc="d_wc")

        t_i1 = p.add("gpsimd", lambda g: g.iota(
            eye_sb[:, :], pattern=[[1, 128]], base=0, channel_multiplier=-1,
            allow_small_or_imprecise_dtypes=True), inc="sg")
        t_hc0 = p.add("gpsimd", lambda g: g.memset(hcol_sb[:, :, :], 0.0),
                      inc="sg")
        t_eye = p.add("vector", lambda v: v.tensor_single_scalar(
            eye_sb[:, :], eye_sb[:, :], 0.0, ALU.is_equal),
            waits=[t_i1], inc="sv")
        t_sel = t_eye

        with ExitStack() as ph1:
            E1 = ph1.enter_context
            xnT = E1(nc.sbuf_tensor("xnT", [128, G, L], BF16))
            wbcT_sb = E1(nc.sbuf_tensor("wbcT_sb", [128, G, 128], BF16))
            A2_sb = E1(nc.sbuf_tensor("A2_sb", [128, 4, NST], F32))
            statsr_sb = E1(nc.sbuf_tensor("statsr_sb", [128, TB], F32))
            statsn_sb = E1(nc.sbuf_tensor("statsn_sb", [128, TB], F32))
            onesf_sb = E1(nc.sbuf_tensor("onesf_sb", [1, 128], F32))
            onesb_sb = E1(nc.sbuf_tensor("onesb_sb", [1, 128], BF16))
            stt_tmp = E1(nc.sbuf_tensor("stt_tmp", [128, 16], F32))
            _xt = E1(nc.sbuf_tensor("xt0", [128, D], F32))
            _xnb = E1(nc.sbuf_tensor("xnb0", [128, D], BF16))
            _opre = E1(nc.sbuf_tensor("opre0", [128, D], F32))
            xt = [_xt, _xt]
            xnb = [_xnb, _xnb]
            opre = [_opre, _opre]
            drep_sb = E1(nc.sbuf_tensor("drep_sb", [128, D], BF16))
            drow_sb = E1(nc.sbuf_tensor("drow_sb", [1, 256], BF16))
            dBC_sb = E1(nc.sbuf_tensor("dBC_sb", [128, L], F32))
            tki_sb = E1(nc.sbuf_tensor("tki_sb", [128, TB, 4], I32))
            tkif_sb = E1(nc.sbuf_tensor("tkif_sb", [128, TB, 4], F32))
            tkw_sb = E1(nc.sbuf_tensor("tkw_sb", [128, TB, 4], F32))
            cmp_sb = E1(nc.sbuf_tensor("cmp_sb", [128, TB, 4], F32))
            re_sb = E1(nc.sbuf_tensor("re_sb", [128, TB], F32))
            reb_sb = E1(nc.sbuf_tensor("reb_sb", [128, TB], BF16))
            ret_sb = E1(nc.sbuf_tensor("ret_sb", [16, ELOC * 128], BF16))

            p.add("gpsimd", lambda g: g.dma_start(
                wbcT_sb[:, :, :],
                bass.AP(wbcT_d, 0, [[128, 128], [128 * 128, G], [1, 128]])),
                inc="d_wc")
            t_wbc = t_wdt = ("d_wc", p.cnt["d_wc"])
            p.add("gpsimd", lambda g: g.memset(onesf_sb[:, :], 1.0), inc="sg")
            p.add("gpsimd", lambda g: g.memset(onesb_sb[:, :], 1.0), inc="sg")
            t_a = None
            for qa in range(4):
                ta1 = p.add("scalar", lambda s, qa=qa: s.activation(
                    A2_sb[:, :, :], A_sb[:, qa * 4:(qa + 1) * 4, :], AF.Exp),
                    waits=[t_al, t_a], inc="ss")
                t_a = p.add("vector", lambda v, qa=qa: v.tensor_single_scalar(
                    A_sb[:, qa * 4:(qa + 1) * 4, :], A2_sb[:, :, :], -1.0,
                    ALU.mult), waits=[ta1], inc="sv")

            psx_rd = None
            for q in range(8):
                tdr = p.add("gpsimd", lambda g, q=q: g.dma_start(
                    drow_sb[:, :],
                    bass.AP(dp_d, q * 256, [[D, 1], [1, 256]])),
                    waits=[psx_rd], inc="d_dr")
                mmdr = p.add("tensor", lambda t, q=q: t.matmul(
                    ps_x[:, 0:256], onesb_sb[:, :],
                    drow_sb[:, :], start=True, stop=True),
                    waits=[tdr, psx_rd], inc="st")
                psx_rd = p.add("scalar", lambda s, q=q: s.activation(
                    drep_sb[:, q * 256:(q + 1) * 256], ps_x[:, 0:256],
                    AF.Copy), waits=[mmdr], inc="ss")
            t_drep = psx_rd

            def ln_stats(tb, par, src_sb, dummy_sb, rst, nbt, extra_w=()):
                o = par * 8
                t1 = p.add("scalar", lambda s: s.activation(
                    dummy_sb[:, :], src_sb[:, :], AF.Square,
                    accum_out=stt_tmp[:, o:o + 1]),
                    waits=list(extra_w), inc="ss")
                t2 = p.add("scalar", lambda s: s.activation(
                    dummy_sb[:, :], src_sb[:, :], AF.Copy,
                    accum_out=stt_tmp[:, o + 1:o + 2]), waits=[t1], inc="ss")
                t3 = p.add("vector", lambda v: v.tensor_single_scalar(
                    stt_tmp[:, o + 2:o + 3], stt_tmp[:, o + 1:o + 2],
                    1.0 / D, ALU.mult), waits=[t2], inc="sv")
                t3 = p.add("vector", lambda v: v.tensor_single_scalar(
                    stt_tmp[:, o + 3:o + 4], stt_tmp[:, o:o + 1],
                    1.0 / D, ALU.mult), waits=[t3], inc="sv")
                t3 = p.add("vector", lambda v: v.tensor_tensor(
                    stt_tmp[:, o + 4:o + 5], stt_tmp[:, o + 2:o + 3],
                    stt_tmp[:, o + 2:o + 3], ALU.mult), waits=[t3], inc="sv")
                t3 = p.add("vector", lambda v: v.tensor_tensor(
                    stt_tmp[:, o + 3:o + 4], stt_tmp[:, o + 3:o + 4],
                    stt_tmp[:, o + 4:o + 5], ALU.subtract),
                    waits=[t3], inc="sv")
                t3 = p.add("vector", lambda v: v.tensor_single_scalar(
                    stt_tmp[:, o + 3:o + 4], stt_tmp[:, o + 3:o + 4],
                    1e-5, ALU.add), waits=[t3], inc="sv")
                t4 = p.add("scalar", lambda s: s.activation(
                    stt_tmp[:, o + 4:o + 5], stt_tmp[:, o + 3:o + 4],
                    AF.Ln), waits=[t3], inc="ss")
                t5 = p.add("scalar", lambda s: s.activation(
                    rst[:, tb:tb + 1], stt_tmp[:, o + 4:o + 5],
                    AF.Exp, scale=-0.5), waits=[t4], inc="ss")
                t6 = p.add("vector", lambda v: v.scalar_tensor_tensor(
                    nbt[:, tb:tb + 1], stt_tmp[:, o + 2:o + 3], -1.0,
                    rst[:, tb:tb + 1], ALU.mult, ALU.mult),
                    waits=[t5], inc="sv")
                return t6

            class Shared1(list):
                def __setitem__(self, i, v):
                    super().__setitem__(0, v)
                    super().__setitem__(1, v)
            xt_guard = Shared1([None, None])
            xn_guard = Shared1([None, None])
            opre_guard = Shared1([None, None])
            pst_rd = None
            last_tr = None
            for tb in range(TB):
                par = tb % 2
                tld = p.add("sync", lambda s, tb=tb: s.dma_start(
                    xt[par][:, :],
                    bass.AP(xb_d, tb * 128 * D, [[D, 128], [1, D]])),
                    waits=[xt_guard[par]], inc="d_x")
                t6 = ln_stats(tb, par, xt[par], xnb[par], statsr_sb,
                              statsn_sb, extra_w=[tld, xn_guard[par]])
                txn = p.add("scalar", lambda s, tb=tb, par=par: s.activation(
                    xnb[par][:, :], xt[par][:, :], AF.Identity,
                    scale=statsr_sb[:, tb:tb + 1],
                    bias=statsn_sb[:, tb:tb + 1]), waits=[t6], inc="ss")
                for dq in range(4):
                    mm = None
                    for k in range(4):
                        dblk = dq * 4 + k
                        mm = p.add("tensor", lambda t, dblk=dblk, k=k, par=par:
                            t.matmul(
                                ps_t[:, k * 128:(k + 1) * 128],
                                xnb[par][:, dblk * 128:(dblk + 1) * 128],
                                eye_sb[:, :], start=True, stop=True),
                            waits=[txn, t_eye, pst_rd] if k == 0 else [],
                            inc="st")
                    pst_rd = p.add("scalar", lambda s, dq=dq, tb=tb:
                        s.activation(
                            bass.AP(xnT, (dq * 4) * L + tb * 128,
                                    [[G * L, 128], [L, 4], [1, 128]]),
                            ps_t[:, :], AF.Copy), waits=[mm], inc="ss")
                last_tr = pst_rd
                t8 = p.add("vector", lambda v, par=par: v.tensor_tensor(
                    opre[par][:, :], xnb[par][:, :], drep_sb[:, :], ALU.mult),
                    waits=[txn, t_drep, opre_guard[par]], inc="sv")
                t9 = p.add("vector", lambda v, par=par: v.tensor_tensor(
                    opre[par][:, :], opre[par][:, :], xt[par][:, :], ALU.add),
                    waits=[t8], inc="sv")
                opre_guard[par] = p.add("sync", lambda s, tb=tb, par=par:
                    s.dma_start(
                        bass.AP(out_d, tb * 128 * D, [[D, 128], [1, D]]),
                        opre[par][:, :]), waits=[t9], inc="d_opre")
                xt_guard[par] = t9
                xn_guard[par] = pst_rd
                xn_guard[par] = t8 if False else pst_rd
                xn_guard[par] = t8
            out_pre_done = ("d_opre", p.cnt["d_opre"])
            tr_all = ("ss", p.cnt["ss"])

            for tb in range(TB):
                par = tb % 2
                tld0 = p.add("sync", lambda s, tb=tb, par=par: s.dma_start(
                    xt[par][:, :],
                    bass.AP(x0_d, tb * 128 * D, [[D, 128], [1, D]])),
                    waits=[xt_guard[par]], inc="d_x")
                t6 = ln_stats(tb, par, xt[par], xnb[par], stat0r_sb,
                              stat0n_sb, extra_w=[tld0, xn_guard[par], tr_all])
                xt_guard[par] = t6
                xn_guard[par] = t6

            dbc_ev = None
            for tseg in range(4):
                mm = None
                for dblk in range(G):
                    mm = p.add("tensor", lambda t, dblk=dblk, tseg=tseg:
                        t.matmul(
                            ps_x[:, :], wbcT_sb[:, dblk, :],
                            bass.AP(xnT, dblk * L + tseg * 512,
                                    [[G * L, 128], [1, 512]]),
                            start=(dblk == 0), stop=(dblk == G - 1)),
                        waits=[last_tr, t_wbc, t_drep, dbc_ev] if dblk == 0 else [],
                        inc="st")
                dbc_ev = p.add("scalar", lambda s, tseg=tseg: s.activation(
                    dBC_sb[:, tseg * 512:(tseg + 1) * 512], ps_x[:, :],
                    AF.Copy), waits=[mm], inc="ss")
            t_dtr = p.add("scalar", lambda s: s.activation(
                dtr_sb[:, :], dBC_sb[0:RK, :], AF.Copy),
                waits=[dbc_ev], inc="ss")
            p.add("gpsimd", lambda g: g.dma_start(
                B_sb[:, :], dBC_sb[RK:RK + NST, :]), waits=[dbc_ev],
                inc="d_bc")
            p.add("gpsimd", lambda g: g.dma_start(
                C_sb[:, :], dBC_sb[RK + NST:128, :]), waits=[dbc_ev],
                inc="d_bc")
            t_B = t_C = ("d_bc", p.cnt["d_bc"])

            sp_tmp = _opre[:, 0:512]
            dsg_sb = _opre[:, 512:1024]
            psab_guard = [None, None]
            sp_war = None
            dsg_war = None
            for g_i in range(G):
                for tseg in range(4):
                    par = (g_i * 4 + tseg) % 2
                    mm = p.add("tensor",
                        lambda t, g_i=g_i, tseg=tseg, par=par: t.matmul(
                            PSAB[par][:, :], wdtT_sb[:, g_i, :],
                            dtr_sb[:, tseg * 512:(tseg + 1) * 512],
                            start=True, stop=True),
                        waits=[t_dtr, t_wdt, psab_guard[par]], inc="st")
                    s1 = p.add("scalar", lambda s, g_i=g_i, par=par:
                        s.activation(
                            sp_tmp, PSAB[par][:, :], AF.Exp,
                            bias=bdt_sb[:, g_i:g_i + 1]),
                        waits=[mm, t_bd, opre_guard[0], sp_war], inc="ss")
                    psab_guard[par] = s1
                    s2 = p.add("scalar", lambda s: s.activation(
                        dsg_sb, sp_tmp, AF.Ln, bias=1.0),
                        waits=[s1, dsg_war], inc="ss")
                    sp_war = s2
                    dsg_war = p.add("vector", lambda v, g_i=g_i, tseg=tseg:
                        v.tensor_tensor(
                            bass.AP(du_sb, g_i * L + tseg * 512,
                                    [[G * L, 128], [1, 512]]),
                            dsg_sb,
                            bass.AP(xnT, g_i * L + tseg * 512,
                                    [[G * L, 128], [1, 512]]),
                            ALU.mult), waits=[s2], inc="sv")
            du_done = ("sv", p.cnt["sv"])

            tl1 = p.add("sync", lambda s: s.dma_start(
                tki_sb[:, :, :],
                bass.AP(tki_d, 0, [[4, 128], [512, TB], [1, 4]])),
                inc="d_tk")
            tl2 = p.add("sync", lambda s: s.dma_start(
                tkw_sb[:, :, :],
                bass.AP(tkw_d, 0, [[4, 128], [512, TB], [1, 4]])),
                inc="d_tk")
            tl1 = tl2 = ("d_tk", p.cnt["d_tk"])
            tc1 = p.add("vector", lambda v: v.tensor_copy(
                tkif_sb[:, :, :], tki_sb[:, :, :]), waits=[tl1], inc="sv")
            rrep_done = None
            psx_guard = rrep_done
            reb_war = None
            for e in range(ELOC):
                c1 = p.add("vector", lambda v, e=e: v.tensor_single_scalar(
                    cmp_sb[:, :, :], tkif_sb[:, :, :], ecol_sb[:, e:e + 1],
                    ALU.is_equal), waits=[tc1, t_ec, tl2], inc="sv")
                c2 = p.add("vector", lambda v: v.tensor_tensor(
                    cmp_sb[:, :, :], cmp_sb[:, :, :], tkw_sb[:, :, :],
                    ALU.mult), waits=[c1], inc="sv")
                c3 = p.add("vector", lambda v, e=e: v.tensor_reduce(
                    re_sb[:, :], cmp_sb[:, :, :], mybir.AxisListType.X,
                    ALU.add), waits=[c2], inc="sv")
                c4 = p.add("vector", lambda v: v.tensor_copy(
                    reb_sb[:, :], re_sb[:, :]), waits=[c3, reb_war],
                    inc="sv")
                mmt = p.add("tensor", lambda t: t.matmul(
                    ps_x[0:16, 0:128], reb_sb[:, :], eye_sb[:, :],
                    start=True, stop=True),
                    waits=[c4, dbc_ev, t_eye, psx_guard], inc="st")
                reb_war = mmt
                evt = p.add("scalar", lambda s, e=e: s.activation(
                    ret_sb[0:16, e * 128:(e + 1) * 128], ps_x[0:16, 0:128],
                    AF.Copy), waits=[mmt], inc="ss")
                for tq4 in range(4):
                    mm = None
                    for k in range(4):
                        tb_i = tq4 * 4 + k
                        mm = p.add("tensor", lambda t, tb_i=tb_i, k=k, e=e:
                            t.matmul(
                                ps_x[:, k * 128:(k + 1) * 128],
                                bass.AP(eye_sb, tb_i, [[128, 16], [0, 128]]),
                                ret_sb[0:16, e * 128:(e + 1) * 128],
                                start=True, stop=True),
                            waits=[evt, t_sel, psx_guard] if k == 0
                            else [], inc="st")
                    psx_guard = p.add("scalar", lambda s, e=e, tq4=tq4:
                        s.activation(
                            bass.AP(rrep_sb, e * L + tq4 * 512,
                                    [[ELOC * L, 128], [1, 512]]),
                            ps_x[:, :], AF.Copy), waits=[mm], inc="ss")
                rrep_done = psx_guard
        p.barrier()

        _yst = E(nc.sbuf_tensor("ystage0", [128, TQ], F32))
        _yst1 = E(nc.sbuf_tensor("ystage1", [128, TQ], F32))
        _fst = E(nc.sbuf_tensor("fstage0", [128, TQ], F32))
        ystage = [_yst, _yst1]
        fstage = [_fst, _fst]
        a8 = [E(nc.sbuf_tensor(f"a8_{i}", [128, NBS, TQ], BF16))
              for i in range(2)]
        b8 = [E(nc.sbuf_tensor(f"b8_{i}", [128, NBS, TQ], BF16))
              for i in range(2)]
        B8_sb = E(nc.sbuf_tensor("B8_sb", [128, NBS, TQ], BF16))
        C8_sb = E(nc.sbuf_tensor("C8_sb", [128, NBS, TQ], BF16))
        _dgc = E(nc.sbuf_tensor("dgc0", [128, TQ], F32))
        dgc_sb = [_dgc, _dgc]
        sp2_sb = E(nc.sbuf_tensor("sp2_sb", [128, TQ], F32))
        cfix_sb = E(nc.sbuf_tensor("cfix_sb", [128, NBS], F32))
        cfix2_sb = E(nc.sbuf_tensor("cfix2_sb", [128, NBS], F32))
        H8_sb = E(nc.sbuf_tensor("H8_sb", [128, NBS, TQ], BF16))
        xq_sb = E(nc.sbuf_tensor("xq_sb", [128, G, TQ], BF16))
        actq_sb = E(nc.sbuf_tensor("actq_sb", [128, IBLK, TQ], BF16))
        wgs_sb = E(nc.sbuf_tensor("wgs_sb", [128, G, 128], BF16))
        wus_sb = E(nc.sbuf_tensor("wus_sb", [128, G, 128], BF16))
        wds_sb = E(nc.sbuf_tensor("wds_sb", [128, IBLK, 128], BF16))
        seg_sb = E(nc.sbuf_tensor("seg_sb", [128, TQ], F32))
        sil_sb = E(nc.sbuf_tensor("sil_sb", [128, TQ], F32))
        x0q_sb = E(nc.sbuf_tensor("x0q_sb", [128, 1024], BF16))
        xnq_sb = E(nc.sbuf_tensor("xnq_sb", [128, 1024], BF16))

        class _Shf(list):
            def __setitem__(self, i, v):
                super().__setitem__(0, v)
                super().__setitem__(1, v)
        st_ffn = {
            "xq_ready": None, "xq_users": [],
            "wg_tok": None, "wu_tok": None, "wd_tok": None,
            "act_toks": [None] * IBLK, "act_users": [],
            "last_e0_dma": None, "last_dma": None,
            "psg_rd": None, "psu_rd": None, "psd_rd": None,
            "pst_rd": ("ss", p.cnt["ss"]),
            "seg_guard": None, "sil_guard": None,
            "fst_par": 0, "fst_guard": _Shf([None, None]),
            "cc_tok": None,
        }

        def _emit_pc(dst, srct, e, q, rows, rowlen):
            p.add("gpsimd", lambda g, e=e, q=q: g.dma_start(
                bass.AP(dst, e * rows * 4 * rowlen + q * rows * rowlen,
                        [[rowlen, rows], [1, rowlen]]),
                bass.AP(srct, e * rows * 4 * rowlen + q * rows * rowlen,
                        [[rowlen, rows], [1, rowlen]])), inc="d_tp")
        for e in range(ELOC):
            for q in range(4):
                _emit_pc(wgb_d, wg_d, e, q, D // 4, FI)
                _emit_pc(wub_d, wu_d, e, q, D // 4, FI)
                _emit_pc(wdb_d, wd_d, e, q, FI // 4, D)
        pc_all = ("d_tp", p.cnt["d_tp"])

        def u_xq(tq):
            def emit():
                prev_users = list(st_ffn["xq_users"])
                st_ffn["xq_users"] = []
                prev_sxn = None
                for blk in range(4):
                    tbq = tq * 4 + blk
                    for dh in range(2):
                        tok_ld = p.add("gpsimd",
                            lambda g, tbq=tbq, dh=dh: g.dma_start(
                                x0q_sb[:, :],
                                bass.AP(x0_d, tbq * 128 * D + dh * 1024,
                                        [[D, 128], [1, 1024]])),
                            waits=[st_ffn["xq_ready"], prev_sxn],
                            inc="d_x0q")
                        sxn = p.add("scalar", lambda s, tbq=tbq: s.activation(
                            xnq_sb[:, :], x0q_sb[:, :], AF.Identity,
                            scale=stat0r_sb[:, tbq:tbq + 1],
                            bias=stat0n_sb[:, tbq:tbq + 1]),
                            waits=[tok_ld], inc="ss")
                        prev_sxn = sxn
                        for dq in range(2):
                            mm = None
                            for k in range(4):
                                dblk8 = dq * 4 + k
                                mm = p.add("tensor",
                                    lambda t, dblk8=dblk8, k=k: t.matmul(
                                        ps_t[:, k * 128:(k + 1) * 128],
                                        xnq_sb[:, dblk8 * 128:(dblk8 + 1) * 128],
                                        eye_sb[:, :], start=True, stop=True),
                                    waits=([sxn, st_ffn["pst_rd"]] + prev_users)
                                    if k == 0 else [],
                                    inc="st")
                            prev_users = []
                            st_ffn["pst_rd"] = p.add("scalar",
                                lambda s, dq=dq, dh=dh, blk=blk: s.activation(
                                    bass.AP(xq_sb,
                                            (dh * 8 + dq * 4) * TQ + blk * 128,
                                            [[G * TQ, 128], [TQ, 4], [1, 128]]),
                                    ps_t[:, :], AF.Copy),
                                waits=[mm], inc="ss")
                st_ffn["xq_ready"] = st_ffn["pst_rd"]
            return emit

        def u_gateup(tq, e, ib):
            def emit():
                wg_tok = p.add("sync", lambda s: s.dma_start(
                    wgs_sb[:, :, :],
                    bass.AP(wgb_d, e * D * FI + ib * 128,
                            [[FI, 128], [128 * FI, G], [1, 128]])),
                    waits=[st_ffn["wg_tok"], pc_all], inc="d_wg")
                wu_tok = p.add("sync", lambda s: s.dma_start(
                    wus_sb[:, :, :],
                    bass.AP(wub_d, e * D * FI + ib * 128,
                            [[FI, 128], [128 * FI, G], [1, 128]])),
                    waits=[st_ffn["wu_tok"], pc_all], inc="d_wu")
                mmg = mmu = None
                for hb in range(G):
                    mmg = p.add("tensor", lambda t, hb=hb: t.matmul(
                        ps_g[:, :], wgs_sb[:, hb, :],
                        bass.AP(xq_sb, hb * TQ, [[G * TQ, 128], [1, TQ]]),
                        start=(hb == 0), stop=(hb == G - 1)),
                        waits=[wg_tok, st_ffn["xq_ready"], st_ffn["psg_rd"]]
                        if hb == 0 else [],
                        inc="st")
                for hb in range(G):
                    mmu = p.add("tensor", lambda t, hb=hb: t.matmul(
                        ps_u[:, :], wus_sb[:, hb, :],
                        bass.AP(xq_sb, hb * TQ, [[G * TQ, 128], [1, TQ]]),
                        start=(hb == 0), stop=(hb == G - 1)),
                        waits=[wu_tok, st_ffn["psu_rd"]] if hb == 0 else [],
                        inc="st")
                st_ffn["wg_tok"] = mmg
                st_ffn["wu_tok"] = mmu
                st_ffn["xq_users"] = [mmu]
                s1 = p.add("scalar", lambda s: s.activation(
                    seg_sb[:, :], ps_g[:, :], AF.Exp, scale=-1.0),
                    waits=[mmg, st_ffn["seg_guard"]], inc="ss")
                s2 = p.add("scalar", lambda s: s.activation(
                    sil_sb[:, :], seg_sb[:, :], AF.Ln, bias=1.0),
                    waits=[s1, st_ffn["sil_guard"]], inc="ss")
                s3 = p.add("scalar", lambda s: s.activation(
                    seg_sb[:, :], sil_sb[:, :], AF.Exp, scale=-1.0),
                    waits=[s2], inc="ss")
                v1 = p.add("vector", lambda v: v.tensor_tensor(
                    sil_sb[:, :], seg_sb[:, :], ps_g[:, :], ALU.mult),
                    waits=[s3], inc="sv")
                st_ffn["seg_guard"] = v1
                st_ffn["psg_rd"] = v1
                v2 = p.add("vector", lambda v: v.tensor_tensor(
                    sil_sb[:, :], sil_sb[:, :], ps_u[:, :], ALU.mult),
                    waits=[v1, mmu], inc="sv")
                st_ffn["psu_rd"] = v2
                v3 = p.add("vector", lambda v, e=e, tq=tq, ib=ib:
                    v.tensor_tensor(
                        bass.AP(actq_sb, ib * TQ, [[IBLK * TQ, 128], [1, TQ]]),
                        sil_sb[:, :],
                        bass.AP(rrep_sb, e * L + tq * TQ,
                                [[ELOC * L, 128], [1, TQ]]),
                        ALU.mult),
                    waits=[v2, rrep_done] + st_ffn["act_users"], inc="sv")
                st_ffn["sil_guard"] = v3
                st_ffn["act_users"] = []
                st_ffn["act_toks"][ib] = v3
            return emit

        def u_down(tq, e, hb):
            def emit():
                wd_tok = p.add("sync", lambda s: s.dma_start(
                    wds_sb[:, :, :],
                    bass.AP(wdb_d, e * FI * D + hb * 128,
                            [[D, 128], [128 * D, IBLK], [1, 128]])),
                    waits=[st_ffn["wd_tok"], pc_all], inc="d_wd")
                mmd = None
                for ib in range(IBLK):
                    mmd = p.add("tensor", lambda t, ib=ib: t.matmul(
                        ps_d[:, :], wds_sb[:, ib, :],
                        bass.AP(actq_sb, ib * TQ, [[IBLK * TQ, 128], [1, TQ]]),
                        start=(ib == 0), stop=(ib == IBLK - 1)),
                        waits=([wd_tok, st_ffn["psd_rd"]] +
                               [t for t in st_ffn["act_toks"] if t])
                        if ib == 0 else [],
                        inc="st")
                st_ffn["wd_tok"] = mmd
                if hb == G - 1:
                    st_ffn["act_users"] = [mmd]
                par = st_ffn["fst_par"]
                st_ffn["fst_par"] = 1 - par
                ev = p.add("scalar", lambda s, par=par: s.activation(
                    fstage[par][:, :], ps_d[:, :], AF.Copy),
                    waits=[mmd, st_ffn["fst_guard"][par]], inc="ss")
                st_ffn["psd_rd"] = ev
                if e == 0:
                    tok = p.add("sync", lambda s, par=par, hb=hb, tq=tq:
                        s.dma_start(
                            bass.AP(ffn_in_d, hb * 128 * L + tq * TQ,
                                    [[L, 128], [1, TQ]]),
                            fstage[par][:, :]),
                        waits=[ev, st_ffn["fst_guard"][0]], inc="d_f")
                else:
                    tok = p.add("gpsimd", lambda g, par=par, hb=hb, tq=tq:
                        g.dma_start(
                            bass.AP(ffn_in_d, hb * 128 * L + tq * TQ,
                                    [[L, 128], [1, TQ]]),
                            fstage[par][:, :], accum_op=ALU.add),
                        waits=[ev, st_ffn["last_e0_dma"],
                               st_ffn["fst_guard"][0]], inc="d_f")
                st_ffn["last_dma"] = tok
                st_ffn["fst_guard"][par] = tok
            return emit

        ffn_units = []
        for tq in range(4) if not _DBG["idle_ffn"] else []:
            ffn_units.append(u_xq(tq))
            for e in range(ELOC):
                for ib in range(IBLK):
                    ffn_units.append(u_gateup(tq, e, ib))
                for hb in range(G):
                    ffn_units.append(u_down(tq, e, hb))
                if e == 0:
                    def mark_e0():
                        st_ffn["last_e0_dma"] = st_ffn["last_dma"]
                    ffn_units.append(mark_e0)

        def coll_unit():
            st_ffn["cc_tok"] = p.add("gpsimd", lambda g: g.collective_compute(
                "AllReduce", ALU.add,
                replica_groups=[list(range(NCORE))],
                ins=[ffn_in_d.ap()], outs=[ffn_out_d.ap()]),
                waits=[st_ffn["last_dma"]], inc="cc", amt=1)
        if not _DBG["idle_ffn"]:
            ffn_units.append(coll_unit)

        bc_guard = None
        psy_rd = None
        sp2_war = None
        psab_guard = [psab_guard[0], psab_guard[1]]
        a_guard = [None, None]
        b_guard = [None, None]
        dgc_guard = [None, None]
        yst_guard = [None, None]
        ych = ["d_y", "d_y1", "d_y2", "d_y3"]
        ylast = [None, None, None, None]
        ybody = [0]
        wave_tok = None
        ffn_i = 0

        def emit_A(cq, nb, g_i):
            nonlocal sp2_war
            par = g_i % 2
            mm = p.add("tensor", lambda t, g_i=g_i, cq=cq, par=par:
                t.matmul(PSAB[par][:, :], wdtT_sb[:, g_i, :],
                         dtr_sb[:, cq * TQ:(cq + 1) * TQ],
                         start=True, stop=True),
                waits=[psab_guard[par]], inc="st")
            s1 = p.add("scalar", lambda s, g_i=g_i, par=par:
                s.activation(sp2_sb[:, :], PSAB[par][:, :], AF.Exp,
                             bias=bdt_sb[:, g_i:g_i + 1]),
                waits=[mm, sp2_war], inc="ss")
            psab_guard[par] = s1
            s2 = p.add("scalar", lambda s, par=par: s.activation(
                dgc_sb[par][:, :], sp2_sb[:, :], AF.Ln, bias=1.0),
                waits=[s1, dgc_guard[par]], inc="ss")
            sp2_war = s2
            atoks = []
            for n in range(NBS):
                ng = nb * NBS + n
                atoks.append(p.add("scalar",
                    lambda s, n=n, ng=ng, g_i=g_i, par=par:
                    s.activation(
                        bass.AP(a8[par], n * TQ,
                                [[NBS * TQ, 128], [1, TQ]]),
                        dgc_sb[par][:, :], AF.Exp,
                        scale=A_sb[:, g_i, ng:ng + 1]),
                    waits=[s2, t_a, a_guard[par]] if n == 0
                    else [], inc="ss"))
            dgc_guard[par] = atoks[-1]
            return atoks[-1]

        def emit_B(cq, nb, g_i, a_ready, bc_ready):
            nonlocal bc_guard, psy_rd
            par = g_i % 2
            v1 = p.add("vector", lambda v, par=par, g_i=g_i, nb=nb:
                v.tensor_tensor(
                    cfix_sb[:, :],
                    bass.AP(a8[par], 0, [[NBS * TQ, 128], [TQ, NBS]]),
                    bass.AP(hcol_sb, g_i * NST + nb * NBS,
                            [[G * NST, 128], [1, NBS]]),
                    ALU.mult),
                waits=[a_ready, t_hc0], inc="sv")
            v2 = p.add("vector", lambda v, par=par: v.memset(
                bass.AP(a8[par], 0,
                        [[NBS * TQ, 128], [TQ, NBS], [1, 1]]),
                0.0), waits=[v1], inc="sv")
            v3 = p.add("vector", lambda v, par=par, g_i=g_i, cq=cq:
                v.tensor_tensor(
                    b8[par][:, :, :],
                    bass.AP(du_sb, g_i * L + cq * TQ,
                            [[G * L, 128], [0, NBS], [1, TQ]]),
                    B8_sb[:, :, :], ALU.mult),
                waits=[bc_ready, du_done, v2, b_guard[par]],
                inc="sv")
            v4 = p.add("vector", lambda v, par=par: v.tensor_tensor(
                cfix2_sb[:, :], cfix_sb[:, :],
                bass.AP(b8[par], 0, [[NBS * TQ, 128], [TQ, NBS]]),
                ALU.add), waits=[v3], inc="sv")
            v4b = p.add("vector", lambda v, par=par: v.tensor_copy(
                bass.AP(b8[par], 0, [[NBS * TQ, 128], [TQ, NBS]]),
                cfix2_sb[:, :]), waits=[v4], inc="sv")
            v5 = p.add("vector", lambda v, par=par: v.tensor_tensor_scan(
                bass.AP(H8_sb, 0, [[NBS * TQ, 128], [1, NBS * TQ]]),
                bass.AP(a8[par], 0, [[NBS * TQ, 128], [1, NBS * TQ]]),
                bass.AP(b8[par], 0, [[NBS * TQ, 128], [1, NBS * TQ]]),
                0.0, ALU.mult, ALU.add), waits=[v4b], inc="sv")
            a_guard[par] = v5
            v6 = p.add("vector", lambda v, g_i=g_i, nb=nb:
                v.tensor_copy(
                    bass.AP(hcol_sb, g_i * NST + nb * NBS,
                            [[G * NST, 128], [1, NBS]]),
                    bass.AP(H8_sb, TQ - 1, [[NBS * TQ, 128], [TQ, NBS]])),
                waits=[v5], inc="sv")
            v7 = p.add("vector", lambda v, par=par: v.tensor_tensor(
                b8[par][:, :, :], H8_sb[:, :, :], C8_sb[:, :, :],
                ALU.mult), waits=[v6], inc="sv")
            bc_guard = v7
            mmy = None
            for n in range(NBS):
                for j in range(4):
                    mmy = p.add("tensor",
                        lambda t, par=par, n=n, j=j: t.matmul(
                            ps_y[:, j * 128:(j + 1) * 128],
                            bass.AP(b8[par], n * TQ + j * 128,
                                    [[NBS * TQ, 128], [1, 128]]),
                            eye_sb[:, :],
                            start=(n == 0 and j == 0),
                            stop=(n == NBS - 1 and j == 3)),
                        waits=[v7, psy_rd] if n == 0 and j == 0
                        else [], inc="st")
            b_guard[par] = mmy
            ypar = g_i % 2
            ev = p.add("scalar", lambda s, ypar=ypar: s.activation(
                ystage[ypar][:, :], ps_y[:, :], AF.Copy),
                waits=[mmy, yst_guard[ypar]], inc="ss")
            psy_rd = ev
            ci = ybody[0] % 4
            ydma = p.add("sync", lambda s, ypar=ypar, cq=cq, nb=nb, g_i=g_i:
                s.dma_start(
                    bass.AP(y_d, (((cq * NBQ + nb) * G) + g_i) * 128 * TQ,
                            [[TQ, 128], [1, TQ]]),
                    ystage[ypar][:, :]),
                waits=[ev, ylast[ci]], inc=ych[ci])
            ybody[0] += 1
            ylast[ci] = ydma
            yst_guard[ypar] = ydma

        for cq in range(CQ):
            for nb in range(NBQ):
                bc_toks = []
                for n in range(NBS):
                    ng = nb * NBS + n
                    par = n % 2
                    mmB = p.add("tensor", lambda t, ng=ng, par=par, cq=cq:
                        t.matmul(
                        PSAB[par][:, :],
                        bass.AP(eye_sb, ng, [[128, NST], [0, 128]]),
                        B_sb[:, cq * TQ:(cq + 1) * TQ],
                        start=True, stop=True),
                        waits=[t_B, t_sel, bc_guard, psab_guard[par]],
                        inc="st")
                    evB = p.add("scalar", lambda s, n=n, par=par: s.activation(
                        bass.AP(B8_sb, n * TQ, [[NBS * TQ, 128], [1, TQ]]),
                        PSAB[par][:, :], AF.Copy), waits=[mmB], inc="ss")
                    mmC = p.add("tensor", lambda t, ng=ng, par=par, cq=cq:
                        t.matmul(
                        PSAB[par][:, :],
                        bass.AP(eye_sb, ng, [[128, NST], [0, 128]]),
                        C_sb[:, cq * TQ:(cq + 1) * TQ],
                        start=True, stop=True),
                        waits=[t_C, evB], inc="st")
                    evC = p.add("scalar", lambda s, n=n, par=par: s.activation(
                        bass.AP(C8_sb, n * TQ, [[NBS * TQ, 128], [1, TQ]]),
                        PSAB[par][:, :], AF.Copy), waits=[mmC], inc="ss")
                    psab_guard[par] = evC
                    bc_toks.append(evC)
                bc_ready = bc_toks[-1]

                pend = None
                for g_i in range(G):
                    a_ready = emit_A(cq, nb, g_i)
                    if pend is not None:
                        emit_B(*pend)
                        if ffn_i < len(ffn_units):
                            ffn_units[ffn_i]()
                            ffn_i += 1
                    pend = (cq, nb, g_i, a_ready, bc_ready)
                emit_B(*pend)
                if ffn_i < len(ffn_units):
                    ffn_units[ffn_i]()
                    ffn_i += 1
                wave_tok = None
        while ffn_i < len(ffn_units):
            ffn_units[ffn_i]()
            ffn_i += 1

        y_alldone = [(c_, p.cnt[c_]) for c_ in ych]
        fstL = [fstage[0], sil_sb]
        ychL = ["d_x", "d_x0q"]
        yldlast = {0: st_ffn["fst_guard"][0], 1: st_ffn["sil_guard"]}
        ob_last = None
        wb_last = None
        tt_last = st_ffn["seg_guard"]
        yli = 0
        for blk in range(TB):
            cq, j = divmod(blk, 4)
            for gq in range(4):
                ob = p.add("sync", lambda s, cq=cq, j=j, gq=gq: s.dma_start(
                    ystage[0][:, :],
                    bass.AP(out_d, (cq * 512 + j * 128) * D + gq * 512,
                            [[D, 128], [1, 512]])),
                    waits=[out_pre_done, tt_last, ob_last] + y_alldone,
                    inc="d_tk")
                ob_last = ob
                for g4 in range(4):
                    g = gq * 4 + g4
                    bi = yli % 2
                    yld = p.add("sync", lambda s, cq=cq, j=j, g=g, bi=bi:
                        s.dma_start(
                            fstL[bi][:, :],
                            bass.AP(y_d,
                                    cq * NBQ * G * 128 * TQ + g * 128 * TQ
                                    + j * 128,
                                    [[TQ, 128], [G * 128 * TQ, NBQ],
                                     [1, 128]])),
                        waits=[yldlast[bi]], inc=ychL[bi])
                    yli += 1
                    t1 = p.add("vector", lambda v, bi=bi: v.tensor_tensor(
                        seg_sb[:, 0:128],
                        bass.AP(fstL[bi], 0, [[512, 128], [1, 128]]),
                        bass.AP(fstL[bi], 128, [[512, 128], [1, 128]]),
                        ALU.add), waits=[yld, ob, tt_last, wb_last],
                        inc="sv")
                    t2 = p.add("vector", lambda v, bi=bi: v.tensor_tensor(
                        seg_sb[:, 128:256],
                        bass.AP(fstL[bi], 256, [[512, 128], [1, 128]]),
                        bass.AP(fstL[bi], 384, [[512, 128], [1, 128]]),
                        ALU.add), waits=[t1], inc="sv")
                    t3 = p.add("vector", lambda v: v.tensor_tensor(
                        seg_sb[:, 256:384], seg_sb[:, 0:128],
                        seg_sb[:, 128:256], ALU.add), waits=[t2], inc="sv")
                    t4 = p.add("vector", lambda v, g4=g4: v.tensor_tensor(
                        ystage[1][:, g4 * 128:(g4 + 1) * 128],
                        ystage[0][:, g4 * 128:(g4 + 1) * 128],
                        seg_sb[:, 256:384], ALU.add), waits=[t3], inc="sv")
                    yldlast[bi] = t3
                    tt_last = t4
                wb_last = p.add("sync", lambda s, cq=cq, j=j, gq=gq:
                    s.dma_start(
                        bass.AP(out_d, (cq * 512 + j * 128) * D + gq * 512,
                                [[D, 128], [1, 512]]),
                        ystage[1][:, :]), waits=[tt_last], inc="d_dr")
        cmb_done = ("d_dr", p.cnt["d_dr"])
        st_ffn["sil_guard"] = tt_last


        tst_ap = bass.AP(sil_sb, 0, [[TQ, 128], [128, 4], [1, 128]])
        fin_iter = range(G) if not _DBG["idle_ffn"] else range(0)
        fin_ap = bass.AP(a8[0], 0, [[NBS * TQ, 128], [1, L]])
        last_fin_dma = None
        for hb in fin_iter:
            tok_ld = p.add("gpsimd", lambda g, hb=hb: g.dma_start(
                fin_ap,
                bass.AP(ffn_out_d, hb * 128 * L, [[L, 128], [1, L]])),
                waits=[st_ffn["cc_tok"], last_fin_dma, a_guard[0],
                       b_guard[0], bc_guard], inc="d_fin")
            for dq in range(4):
                mm = None
                for k in range(4):
                    tblk = dq * 4 + k
                    mm = p.add("tensor", lambda t, tblk=tblk, k=k: t.matmul(
                        ps_t[:, k * 128:(k + 1) * 128],
                        bass.AP(a8[0], tblk * 128, [[NBS * TQ, 128], [1, 128]]),
                        eye_sb[:, :], start=True, stop=True),
                        waits=[tok_ld, st_ffn["pst_rd"]] if k == 0 else [],
                        inc="st")
                ev = p.add("scalar", lambda s, dq=dq: s.activation(
                    tst_ap, ps_t[:, :], AF.Identity,
                    scale=m0_sb[:, 0:1]),
                    waits=[mm, last_fin_dma, st_ffn["sil_guard"]], inc="ss")
                st_ffn["pst_rd"] = ev
                last_fin_dma = p.add("gpsimd", lambda g, hb=hb, dq=dq:
                    g.dma_start(
                        bass.AP(f_out_d, (dq * 4 * 128) * D + hb * 128,
                                [[D, 128], [128 * D, 4], [1, 128]]),
                        tst_ap, accum_op=ALU.add),
                    waits=[ev, last_fin_dma, cmb_done], inc="d_fin")

        p.add("sync", None,
              waits=[(c, v) for c, v in p.cnt.items() if v > 0])

        with nc.Block() as block:
            @block.sync
            def _(e):
                p.play("sync", e, sems)

            @block.gpsimd
            def _(e):
                p.play("gpsimd", e, sems)

            @block.vector
            def _(e):
                p.play("vector", e, sems)

            @block.scalar
            def _(e):
                p.play("scalar", e, sems)

            @block.tensor
            def _(e):
                p.play("tensor", e, sems)

    return nc


def kernel(L0_attn_x, L0_attn_z, L0_attn_W_bc, L0_attn_W_dt, L0_attn_b_dt,
           L0_attn_A_log, L0_attn_D_param, L0_ffn_topk_idx, L0_ffn_topk_weight,
           L0_ffn_gate_weight, L0_ffn_up_weight, L0_ffn_down_weight):
    x = np.ascontiguousarray(np.asarray(L0_attn_x, dtype=np.float32))
    wbcT = np.ascontiguousarray(np.asarray(L0_attn_W_bc, np.float32).T)
    wdtT = np.ascontiguousarray(np.asarray(L0_attn_W_dt, np.float32).T)
    bdt = np.ascontiguousarray(np.asarray(L0_attn_b_dt, np.float32))
    alog = np.ascontiguousarray(np.asarray(L0_attn_A_log, np.float32))
    dp = np.ascontiguousarray(np.asarray(L0_attn_D_param, np.float32))
    tki = np.ascontiguousarray(np.asarray(L0_ffn_topk_idx, np.int32).reshape(L, 4))
    tkw = np.ascontiguousarray(
        np.asarray(L0_ffn_topk_weight, np.float32).reshape(L, 4))
    wgT = np.ascontiguousarray(
        np.asarray(L0_ffn_gate_weight, np.float32).transpose(0, 2, 1))
    wuT = np.ascontiguousarray(
        np.asarray(L0_ffn_up_weight, np.float32).transpose(0, 2, 1))
    wdT = np.ascontiguousarray(
        np.asarray(L0_ffn_down_weight, np.float32).transpose(0, 2, 1))

    nc = build()
    in_maps = []
    for c in range(NCORE):
        in_maps.append(dict(
            xb=np.ascontiguousarray(x[c]),
            x0=np.ascontiguousarray(x[0]),
            wbcT=wbcT, wdtT=wdtT, bdt=bdt, alog=alog, dp=dp,
            tki=tki, tkw=tkw,
            wg=np.ascontiguousarray(wgT[2 * c:2 * c + 2]),
            wu=np.ascontiguousarray(wuT[2 * c:2 * c + 2]),
            wd=np.ascontiguousarray(wdT[2 * c:2 * c + 2]),
            ecol=np.tile(np.array([[2 * c, 2 * c + 1]], np.float32), (128, 1)),
            m0=np.full((128, 1), 1.0 if c == 0 else 0.0, np.float32),
        ))
    res = run_bass_kernel_spmd(nc, in_maps, core_ids=list(range(NCORE)))
    return np.stack([np.asarray(res.results[c]["out"], np.float32)
                     for c in range(NCORE)], axis=0)


# version 09
# speedup vs baseline: 1.3423x; 1.3423x over previous
import numpy as np
import concourse.bass as bass
import concourse.mybir as mybir
from concourse.bass_utils import run_bass_kernel_spmd
from contextlib import ExitStack

F32 = mybir.dt.float32
BF16 = mybir.dt.bfloat16
I32 = mybir.dt.int32
AF = mybir.ActivationFunctionType
ALU = mybir.AluOpType

L = 2048
D = 2048
G = 16
NST = 32
RK = 64
TB = 16
NCORE = 8
NBQ = 4
NBS = 8
CQ = 4
TQ = 512
IBLK = 11
FI = 1408
ELOC = 2

_DBG = {"skip_y": False, "skip_ffn": False, "tap_b8": False,
        "idle_ffn": False, "tap2": False}


class Plan:
    ENGINES = ("sync", "gpsimd", "vector", "scalar", "tensor")

    def __init__(self):
        self.steps = {e: [] for e in self.ENGINES}
        self.cnt = {c: 0 for c in (
            "d_cst", "d_x", "d_opre", "d_tk", "d_wc", "d_dr", "d_bc",
            "d_wg", "d_wu", "d_wd", "d_x0q", "d_y", "d_y1", "d_y2", "d_y3", "d_f", "d_fin", "d_tp",
            "sv", "ss", "st", "sg", "cc")}

    def add(self, eng, emit, waits=(), inc=None, amt=None):
        tok = None
        if inc is not None:
            if amt is None:
                amt = 16 if inc.startswith("d_") else 1
            self.cnt[inc] += amt
            tok = (inc, self.cnt[inc])
        self.steps[eng].append(
            (emit, [w for w in waits if w is not None], inc, amt))
        return tok

    def barrier(self):
        snap = [(c, v) for c, v in self.cnt.items() if v > 0]
        for e in self.ENGINES:
            self.steps[e].append((None, snap, None, None))

    def play(self, eng_name, engine, sems):
        for emit, waits, inc, amt in self.steps[eng_name]:
            for ch, val in waits:
                engine.wait_ge(sems[ch], val)
            inst = emit(engine) if emit is not None else None
            if inc is not None:
                inst.then_inc(sems[inc], amt)


def build():
    nc = bass.Bass("TRN2", target_bir_lowering=False)

    xb_d = nc.dram_tensor("xb", [L, D], F32, kind="ExternalInput")
    x0_d = nc.dram_tensor("x0", [L, D], F32, kind="ExternalInput")
    wbcT_d = nc.dram_tensor("wbcT", [D, 128], F32, kind="ExternalInput")
    wdtT_d = nc.dram_tensor("wdtT", [RK, D], F32, kind="ExternalInput")
    bdt_d = nc.dram_tensor("bdt", [D], F32, kind="ExternalInput")
    alog_d = nc.dram_tensor("alog", [D, NST], F32, kind="ExternalInput")
    dp_d = nc.dram_tensor("dp", [D], F32, kind="ExternalInput")
    tki_d = nc.dram_tensor("tki", [L, 4], I32, kind="ExternalInput")
    tkw_d = nc.dram_tensor("tkw", [L, 4], F32, kind="ExternalInput")
    wg_d = nc.dram_tensor("wg", [ELOC, D, FI], F32, kind="ExternalInput")
    wu_d = nc.dram_tensor("wu", [ELOC, D, FI], F32, kind="ExternalInput")
    wd_d = nc.dram_tensor("wd", [ELOC, FI, D], F32, kind="ExternalInput")
    ecol_d = nc.dram_tensor("ecol", [128, ELOC], F32, kind="ExternalInput")
    m0_d = nc.dram_tensor("m0", [128, 1], F32, kind="ExternalInput")
    out_d = nc.dram_tensor("out", [L, D], F32, kind="ExternalOutput")
    ffn_in_d = nc.dram_tensor("ffn_in", [D, L], F32, kind="Internal")
    wgb_d = nc.dram_tensor("wgb", [ELOC, D, FI], BF16, kind="Internal")
    wub_d = nc.dram_tensor("wub", [ELOC, D, FI], BF16, kind="Internal")
    wdb_d = nc.dram_tensor("wdb", [ELOC, FI, D], BF16, kind="Internal")
    ffn_out_d = nc.dram_tensor("ffn_out", [D, L], F32, kind="Internal")
    dbg_d = nc.dram_tensor("dbg", [L, D], F32, kind="Internal")
    if _DBG["tap2"]:
        tapp_d = nc.dram_tensor("tapp", [7, 128, NBS, TQ], F32,
                                kind="ExternalOutput")
    if _DBG["tap_b8"]:
        tapb_d = nc.dram_tensor("tapb", [16, 128, NBS, TQ], F32,
                                kind="ExternalOutput")
        tapa_d = nc.dram_tensor("tapa", [16, 128, NBS, TQ], F32,
                                kind="ExternalOutput")
        tapy_d = nc.dram_tensor("tapy", [CQ, NBQ, G, 128, TQ], F32,
                                kind="ExternalOutput")
        tapp_d = nc.dram_tensor("tapp", [5, 128, NBS, TQ], F32,
                                kind="ExternalOutput")
    y_out_d = dbg_d if _DBG["skip_y"] else out_d
    f_out_d = dbg_d if _DBG["skip_ffn"] else out_d

    p = Plan()

    with ExitStack() as top:
        E = top.enter_context
        A_sb = E(nc.sbuf_tensor("A_sb", [128, G, NST], F32))
        bdt_sb = E(nc.sbuf_tensor("bdt_sb", [128, G], F32))
        wdtT_sb = E(nc.sbuf_tensor("wdtT_sb", [RK, G, 128], BF16))
        eye_sb = E(nc.sbuf_tensor("eye_sb", [128, 128], BF16))
        dtr_sb = E(nc.sbuf_tensor("dtr_sb", [RK, L], BF16))
        B_sb = E(nc.sbuf_tensor("B_sb", [NST, L], BF16))
        C_sb = E(nc.sbuf_tensor("C_sb", [NST, L], BF16))
        du_sb = E(nc.sbuf_tensor("du_sb", [128, G, L], BF16))
        hcol_sb = E(nc.sbuf_tensor("hcol_sb", [128, G, NST], F32))
        stat0r_sb = E(nc.sbuf_tensor("stat0r_sb", [128, TB], F32))
        stat0n_sb = E(nc.sbuf_tensor("stat0n_sb", [128, TB], F32))
        rrep_sb = E(nc.sbuf_tensor("rrep_sb", [128, ELOC, L], BF16))
        m0_sb = E(nc.sbuf_tensor("m0_sb", [128, 1], F32))
        ecol_sb = E(nc.sbuf_tensor("ecol_sb", [128, ELOC], F32))

        ps_y = E(nc.psum_tensor("ps_y", [128, TQ], F32))
        ps_a = E(nc.psum_tensor("ps_a", [128, 512], F32))
        ps_b = E(nc.psum_tensor("ps_b", [128, 512], F32))
        ps_g = E(nc.psum_tensor("ps_g", [128, 512], F32))
        ps_u = E(nc.psum_tensor("ps_u", [128, 512], F32))
        ps_d = E(nc.psum_tensor("ps_d", [128, 512], F32))
        ps_t = E(nc.psum_tensor("ps_t", [128, 512], F32))
        ps_x = E(nc.psum_tensor("ps_x", [128, 512], F32))
        PSAB = [ps_a, ps_b]

        sems = {ch: E(nc.semaphore(ch)) for ch in p.cnt}

        t_al = p.add("sync", lambda s: s.dma_start(
            A_sb[:, :, :],
            bass.AP(alog_d, 0, [[NST, 128], [128 * NST, G], [1, NST]])),
            inc="d_cst")
        t_bd = p.add("sync", lambda s: s.dma_start(
            bdt_sb[:, :], bass.AP(bdt_d, 0, [[1, 128], [128, G]]),
            allow_slow_non_contiguous=True), inc="d_cst")
        p.add("sync", lambda s: s.dma_start(m0_sb[:, :], m0_d.ap()),
              inc="d_cst")
        t_ec = p.add("sync", lambda s: s.dma_start(
            ecol_sb[:, :], ecol_d.ap()), inc="d_cst")
        tok_cst = ("d_cst", p.cnt["d_cst"])
        t_al = t_bd = t_ec = tok_cst
        t_wdt = p.add("gpsimd", lambda g: g.dma_start(
            wdtT_sb[:, :, :],
            bass.AP(wdtT_d, 0, [[D, RK], [128, G], [1, 128]])), inc="d_wc")

        t_i1 = p.add("gpsimd", lambda g: g.iota(
            eye_sb[:, :], pattern=[[1, 128]], base=0, channel_multiplier=-1,
            allow_small_or_imprecise_dtypes=True), inc="sg")
        t_hc0 = p.add("gpsimd", lambda g: g.memset(hcol_sb[:, :, :], 0.0),
                      inc="sg")
        t_eye = p.add("vector", lambda v: v.tensor_single_scalar(
            eye_sb[:, :], eye_sb[:, :], 0.0, ALU.is_equal),
            waits=[t_i1], inc="sv")
        t_sel = t_eye

        with ExitStack() as ph1:
            E1 = ph1.enter_context
            xnT = E1(nc.sbuf_tensor("xnT", [128, G, L], BF16))
            wbcT_sb = E1(nc.sbuf_tensor("wbcT_sb", [128, G, 128], BF16))
            A2_sb = E1(nc.sbuf_tensor("A2_sb", [128, 4, NST], F32))
            statsr_sb = E1(nc.sbuf_tensor("statsr_sb", [128, TB], F32))
            statsn_sb = E1(nc.sbuf_tensor("statsn_sb", [128, TB], F32))
            onesf_sb = E1(nc.sbuf_tensor("onesf_sb", [1, 128], F32))
            onesb_sb = E1(nc.sbuf_tensor("onesb_sb", [1, 128], BF16))
            stt_tmp = E1(nc.sbuf_tensor("stt_tmp", [128, 16], F32))
            _xt = E1(nc.sbuf_tensor("xt0", [128, D], F32))
            _xnb = E1(nc.sbuf_tensor("xnb0", [128, D], BF16))
            _opre = E1(nc.sbuf_tensor("opre0", [128, D], F32))
            xt = [_xt, _xt]
            xnb = [_xnb, _xnb]
            opre = [_opre, _opre]
            drep_sb = E1(nc.sbuf_tensor("drep_sb", [128, D], BF16))
            drow_sb = E1(nc.sbuf_tensor("drow_sb", [1, 256], BF16))
            dBC_sb = E1(nc.sbuf_tensor("dBC_sb", [128, L], F32))
            tki_sb = E1(nc.sbuf_tensor("tki_sb", [128, TB, 4], I32))
            tkif_sb = E1(nc.sbuf_tensor("tkif_sb", [128, TB, 4], F32))
            tkw_sb = E1(nc.sbuf_tensor("tkw_sb", [128, TB, 4], F32))
            cmp_sb = E1(nc.sbuf_tensor("cmp_sb", [128, TB, 4], F32))
            re_sb = E1(nc.sbuf_tensor("re_sb", [128, TB], F32))
            reb_sb = E1(nc.sbuf_tensor("reb_sb", [128, TB], BF16))
            ret_sb = E1(nc.sbuf_tensor("ret_sb", [16, ELOC * 128], BF16))

            p.add("gpsimd", lambda g: g.dma_start(
                wbcT_sb[:, :, :],
                bass.AP(wbcT_d, 0, [[128, 128], [128 * 128, G], [1, 128]])),
                inc="d_wc")
            t_wbc = t_wdt = ("d_wc", p.cnt["d_wc"])
            p.add("gpsimd", lambda g: g.memset(onesf_sb[:, :], 1.0), inc="sg")
            p.add("gpsimd", lambda g: g.memset(onesb_sb[:, :], 1.0), inc="sg")
            t_a = None
            for qa in range(4):
                ta1 = p.add("scalar", lambda s, qa=qa: s.activation(
                    A2_sb[:, :, :], A_sb[:, qa * 4:(qa + 1) * 4, :], AF.Exp),
                    waits=[t_al, t_a], inc="ss")
                t_a = p.add("vector", lambda v, qa=qa: v.tensor_single_scalar(
                    A_sb[:, qa * 4:(qa + 1) * 4, :], A2_sb[:, :, :], -1.0,
                    ALU.mult), waits=[ta1], inc="sv")

            psx_rd = None
            for q in range(8):
                tdr = p.add("gpsimd", lambda g, q=q: g.dma_start(
                    drow_sb[:, :],
                    bass.AP(dp_d, q * 256, [[D, 1], [1, 256]])),
                    waits=[psx_rd], inc="d_dr")
                mmdr = p.add("tensor", lambda t, q=q: t.matmul(
                    ps_x[:, 0:256], onesb_sb[:, :],
                    drow_sb[:, :], start=True, stop=True),
                    waits=[tdr, psx_rd], inc="st")
                psx_rd = p.add("scalar", lambda s, q=q: s.activation(
                    drep_sb[:, q * 256:(q + 1) * 256], ps_x[:, 0:256],
                    AF.Copy), waits=[mmdr], inc="ss")
            t_drep = psx_rd

            def ln_stats(tb, par, src_sb, dummy_sb, rst, nbt, extra_w=()):
                o = par * 8
                t1 = p.add("scalar", lambda s: s.activation(
                    dummy_sb[:, :], src_sb[:, :], AF.Square,
                    accum_out=stt_tmp[:, o:o + 1]),
                    waits=list(extra_w), inc="ss")
                t2 = p.add("scalar", lambda s: s.activation(
                    dummy_sb[:, :], src_sb[:, :], AF.Copy,
                    accum_out=stt_tmp[:, o + 1:o + 2]), waits=[t1], inc="ss")
                t3 = p.add("vector", lambda v: v.tensor_single_scalar(
                    stt_tmp[:, o + 2:o + 3], stt_tmp[:, o + 1:o + 2],
                    1.0 / D, ALU.mult), waits=[t2], inc="sv")
                t3 = p.add("vector", lambda v: v.tensor_single_scalar(
                    stt_tmp[:, o + 3:o + 4], stt_tmp[:, o:o + 1],
                    1.0 / D, ALU.mult), waits=[t3], inc="sv")
                t3 = p.add("vector", lambda v: v.tensor_tensor(
                    stt_tmp[:, o + 4:o + 5], stt_tmp[:, o + 2:o + 3],
                    stt_tmp[:, o + 2:o + 3], ALU.mult), waits=[t3], inc="sv")
                t3 = p.add("vector", lambda v: v.tensor_tensor(
                    stt_tmp[:, o + 3:o + 4], stt_tmp[:, o + 3:o + 4],
                    stt_tmp[:, o + 4:o + 5], ALU.subtract),
                    waits=[t3], inc="sv")
                t3 = p.add("vector", lambda v: v.tensor_single_scalar(
                    stt_tmp[:, o + 3:o + 4], stt_tmp[:, o + 3:o + 4],
                    1e-5, ALU.add), waits=[t3], inc="sv")
                t4 = p.add("scalar", lambda s: s.activation(
                    stt_tmp[:, o + 4:o + 5], stt_tmp[:, o + 3:o + 4],
                    AF.Ln), waits=[t3], inc="ss")
                t5 = p.add("scalar", lambda s: s.activation(
                    rst[:, tb:tb + 1], stt_tmp[:, o + 4:o + 5],
                    AF.Exp, scale=-0.5), waits=[t4], inc="ss")
                t6 = p.add("vector", lambda v: v.scalar_tensor_tensor(
                    nbt[:, tb:tb + 1], stt_tmp[:, o + 2:o + 3], -1.0,
                    rst[:, tb:tb + 1], ALU.mult, ALU.mult),
                    waits=[t5], inc="sv")
                return t6

            class Shared1(list):
                def __setitem__(self, i, v):
                    super().__setitem__(0, v)
                    super().__setitem__(1, v)
            xt_guard = Shared1([None, None])
            xn_guard = Shared1([None, None])
            opre_guard = Shared1([None, None])
            pst_rd = None
            last_tr = None
            for tb in range(TB):
                par = tb % 2
                tld = p.add("sync", lambda s, tb=tb: s.dma_start(
                    xt[par][:, :],
                    bass.AP(xb_d, tb * 128 * D, [[D, 128], [1, D]])),
                    waits=[xt_guard[par]], inc="d_x")
                t6 = ln_stats(tb, par, xt[par], xnb[par], statsr_sb,
                              statsn_sb, extra_w=[tld, xn_guard[par]])
                txn = p.add("scalar", lambda s, tb=tb, par=par: s.activation(
                    xnb[par][:, :], xt[par][:, :], AF.Identity,
                    scale=statsr_sb[:, tb:tb + 1],
                    bias=statsn_sb[:, tb:tb + 1]), waits=[t6], inc="ss")
                for dq in range(4):
                    mm = None
                    for k in range(4):
                        dblk = dq * 4 + k
                        mm = p.add("tensor", lambda t, dblk=dblk, k=k, par=par:
                            t.matmul(
                                ps_t[:, k * 128:(k + 1) * 128],
                                xnb[par][:, dblk * 128:(dblk + 1) * 128],
                                eye_sb[:, :], start=True, stop=True),
                            waits=[txn, t_eye, pst_rd] if k == 0 else [],
                            inc="st")
                    pst_rd = p.add("scalar", lambda s, dq=dq, tb=tb:
                        s.activation(
                            bass.AP(xnT, (dq * 4) * L + tb * 128,
                                    [[G * L, 128], [L, 4], [1, 128]]),
                            ps_t[:, :], AF.Copy), waits=[mm], inc="ss")
                last_tr = pst_rd
                t8 = p.add("vector", lambda v, par=par: v.tensor_tensor(
                    opre[par][:, :], xnb[par][:, :], drep_sb[:, :], ALU.mult),
                    waits=[txn, t_drep, opre_guard[par]], inc="sv")
                t9 = p.add("vector", lambda v, par=par: v.tensor_tensor(
                    opre[par][:, :], opre[par][:, :], xt[par][:, :], ALU.add),
                    waits=[t8], inc="sv")
                opre_guard[par] = p.add("sync", lambda s, tb=tb, par=par:
                    s.dma_start(
                        bass.AP(out_d, tb * 128 * D, [[D, 128], [1, D]]),
                        opre[par][:, :]), waits=[t9], inc="d_opre")
                xt_guard[par] = t9
                xn_guard[par] = pst_rd
                xn_guard[par] = t8 if False else pst_rd
                xn_guard[par] = t8
            out_pre_done = ("d_opre", p.cnt["d_opre"])
            tr_all = ("ss", p.cnt["ss"])

            for tb in range(TB):
                par = tb % 2
                tld0 = p.add("sync", lambda s, tb=tb, par=par: s.dma_start(
                    xt[par][:, :],
                    bass.AP(x0_d, tb * 128 * D, [[D, 128], [1, D]])),
                    waits=[xt_guard[par]], inc="d_x")
                t6 = ln_stats(tb, par, xt[par], xnb[par], stat0r_sb,
                              stat0n_sb, extra_w=[tld0, xn_guard[par], tr_all])
                xt_guard[par] = t6
                xn_guard[par] = t6

            dbc_ev = None
            for tseg in range(4):
                mm = None
                for dblk in range(G):
                    mm = p.add("tensor", lambda t, dblk=dblk, tseg=tseg:
                        t.matmul(
                            ps_x[:, :], wbcT_sb[:, dblk, :],
                            bass.AP(xnT, dblk * L + tseg * 512,
                                    [[G * L, 128], [1, 512]]),
                            start=(dblk == 0), stop=(dblk == G - 1)),
                        waits=[last_tr, t_wbc, t_drep, dbc_ev] if dblk == 0 else [],
                        inc="st")
                dbc_ev = p.add("scalar", lambda s, tseg=tseg: s.activation(
                    dBC_sb[:, tseg * 512:(tseg + 1) * 512], ps_x[:, :],
                    AF.Copy), waits=[mm], inc="ss")
            t_dtr = p.add("scalar", lambda s: s.activation(
                dtr_sb[:, :], dBC_sb[0:RK, :], AF.Copy),
                waits=[dbc_ev], inc="ss")
            p.add("gpsimd", lambda g: g.dma_start(
                B_sb[:, :], dBC_sb[RK:RK + NST, :]), waits=[dbc_ev],
                inc="d_bc")
            p.add("gpsimd", lambda g: g.dma_start(
                C_sb[:, :], dBC_sb[RK + NST:128, :]), waits=[dbc_ev],
                inc="d_bc")
            t_B = t_C = ("d_bc", p.cnt["d_bc"])

            sp_tmp = _opre[:, 0:512]
            dsg_sb = _opre[:, 512:1024]
            psab_guard = [None, None]
            sp_war = None
            dsg_war = None
            for g_i in range(G):
                for tseg in range(4):
                    par = (g_i * 4 + tseg) % 2
                    mm = p.add("tensor",
                        lambda t, g_i=g_i, tseg=tseg, par=par: t.matmul(
                            PSAB[par][:, :], wdtT_sb[:, g_i, :],
                            dtr_sb[:, tseg * 512:(tseg + 1) * 512],
                            start=True, stop=True),
                        waits=[t_dtr, t_wdt, psab_guard[par]], inc="st")
                    s1 = p.add("scalar", lambda s, g_i=g_i, par=par:
                        s.activation(
                            sp_tmp, PSAB[par][:, :], AF.Exp,
                            bias=bdt_sb[:, g_i:g_i + 1]),
                        waits=[mm, t_bd, opre_guard[0], sp_war], inc="ss")
                    psab_guard[par] = s1
                    s2 = p.add("scalar", lambda s: s.activation(
                        dsg_sb, sp_tmp, AF.Ln, bias=1.0),
                        waits=[s1, dsg_war], inc="ss")
                    sp_war = s2
                    dsg_war = p.add("vector", lambda v, g_i=g_i, tseg=tseg:
                        v.tensor_tensor(
                            bass.AP(du_sb, g_i * L + tseg * 512,
                                    [[G * L, 128], [1, 512]]),
                            dsg_sb,
                            bass.AP(xnT, g_i * L + tseg * 512,
                                    [[G * L, 128], [1, 512]]),
                            ALU.mult), waits=[s2], inc="sv")
            du_done = ("sv", p.cnt["sv"])

            tl1 = p.add("sync", lambda s: s.dma_start(
                tki_sb[:, :, :],
                bass.AP(tki_d, 0, [[4, 128], [512, TB], [1, 4]])),
                inc="d_tk")
            tl2 = p.add("sync", lambda s: s.dma_start(
                tkw_sb[:, :, :],
                bass.AP(tkw_d, 0, [[4, 128], [512, TB], [1, 4]])),
                inc="d_tk")
            tl1 = tl2 = ("d_tk", p.cnt["d_tk"])
            tc1 = p.add("vector", lambda v: v.tensor_copy(
                tkif_sb[:, :, :], tki_sb[:, :, :]), waits=[tl1], inc="sv")
            rrep_done = None
            psx_guard = rrep_done
            reb_war = None
            for e in range(ELOC):
                c1 = p.add("vector", lambda v, e=e: v.tensor_single_scalar(
                    cmp_sb[:, :, :], tkif_sb[:, :, :], ecol_sb[:, e:e + 1],
                    ALU.is_equal), waits=[tc1, t_ec, tl2], inc="sv")
                c2 = p.add("vector", lambda v: v.tensor_tensor(
                    cmp_sb[:, :, :], cmp_sb[:, :, :], tkw_sb[:, :, :],
                    ALU.mult), waits=[c1], inc="sv")
                c3 = p.add("vector", lambda v, e=e: v.tensor_reduce(
                    re_sb[:, :], cmp_sb[:, :, :], mybir.AxisListType.X,
                    ALU.add), waits=[c2], inc="sv")
                c4 = p.add("vector", lambda v: v.tensor_copy(
                    reb_sb[:, :], re_sb[:, :]), waits=[c3, reb_war],
                    inc="sv")
                mmt = p.add("tensor", lambda t: t.matmul(
                    ps_x[0:16, 0:128], reb_sb[:, :], eye_sb[:, :],
                    start=True, stop=True),
                    waits=[c4, dbc_ev, t_eye, psx_guard], inc="st")
                reb_war = mmt
                evt = p.add("scalar", lambda s, e=e: s.activation(
                    ret_sb[0:16, e * 128:(e + 1) * 128], ps_x[0:16, 0:128],
                    AF.Copy), waits=[mmt], inc="ss")
                for tq4 in range(4):
                    mm = None
                    for k in range(4):
                        tb_i = tq4 * 4 + k
                        mm = p.add("tensor", lambda t, tb_i=tb_i, k=k, e=e:
                            t.matmul(
                                ps_x[:, k * 128:(k + 1) * 128],
                                bass.AP(eye_sb, tb_i, [[128, 16], [0, 128]]),
                                ret_sb[0:16, e * 128:(e + 1) * 128],
                                start=True, stop=True),
                            waits=[evt, t_sel, psx_guard] if k == 0
                            else [], inc="st")
                    psx_guard = p.add("scalar", lambda s, e=e, tq4=tq4:
                        s.activation(
                            bass.AP(rrep_sb, e * L + tq4 * 512,
                                    [[ELOC * L, 128], [1, 512]]),
                            ps_x[:, :], AF.Copy), waits=[mm], inc="ss")
                rrep_done = psx_guard
        p.barrier()

        _yst = E(nc.sbuf_tensor("ystage0", [128, TQ], F32))
        _yst1 = E(nc.sbuf_tensor("ystage1", [128, TQ], F32))
        _yst2 = E(nc.sbuf_tensor("ystage2", [128, TQ], F32))
        _fst = E(nc.sbuf_tensor("fstage0", [128, TQ], F32))
        ystage = [_yst, _yst1, _yst2]
        fstage = [_fst, _fst]
        a8 = [E(nc.sbuf_tensor(f"a8_{i}", [128, NBS, TQ], BF16))
              for i in range(2)]
        b8 = [E(nc.sbuf_tensor(f"b8_{i}", [128, NBS, TQ], BF16))
              for i in range(2)]
        B8_sb = E(nc.sbuf_tensor("B8_sb", [128, NBS, TQ], BF16))
        C8_sb = E(nc.sbuf_tensor("C8_sb", [128, NBS, TQ], BF16))
        _dgc = E(nc.sbuf_tensor("dgc0", [128, TQ], F32))
        dgc_sb = [_dgc, _dgc]
        sp2_sb = E(nc.sbuf_tensor("sp2_sb", [128, TQ], F32))
        cfix_sb = E(nc.sbuf_tensor("cfix_sb", [128, NBS], F32))
        cfix2_sb = E(nc.sbuf_tensor("cfix2_sb", [128, NBS], F32))
        H8_sb = E(nc.sbuf_tensor("H8_sb", [128, NBS, TQ], BF16))
        xq_sb = E(nc.sbuf_tensor("xq_sb", [128, G, TQ], BF16))
        actq_sb = E(nc.sbuf_tensor("actq_sb", [128, IBLK, TQ], BF16))
        wgs_sb = E(nc.sbuf_tensor("wgs_sb", [128, G, 128], BF16))
        wus_sb = E(nc.sbuf_tensor("wus_sb", [128, G, 128], BF16))
        wds_sb = E(nc.sbuf_tensor("wds_sb", [128, IBLK, 128], BF16))
        seg_sb = E(nc.sbuf_tensor("seg_sb", [128, TQ], F32))
        sil_sb = E(nc.sbuf_tensor("sil_sb", [128, TQ], F32))
        x0q_sb = E(nc.sbuf_tensor("x0q_sb", [128, 1024], BF16))
        xnq_sb = E(nc.sbuf_tensor("xnq_sb", [128, 1024], BF16))

        class _Shf(list):
            def __setitem__(self, i, v):
                super().__setitem__(0, v)
                super().__setitem__(1, v)
        st_ffn = {
            "xq_ready": None, "xq_users": [],
            "wg_tok": None, "wu_tok": None, "wd_tok": None,
            "act_toks": [None] * IBLK, "act_users": [],
            "last_e0_dma": None, "last_dma": None,
            "psg_rd": None, "psu_rd": None, "psd_rd": None,
            "pst_rd": ("ss", p.cnt["ss"]),
            "seg_guard": None, "sil_guard": None,
            "fst_par": 0, "fst_guard": _Shf([None, None]),
            "cc_tok": None,
        }

        def _emit_pc(dst, srct, e, q, rows, rowlen):
            p.add("gpsimd", lambda g, e=e, q=q: g.dma_start(
                bass.AP(dst, e * rows * 4 * rowlen + q * rows * rowlen,
                        [[rowlen, rows], [1, rowlen]]),
                bass.AP(srct, e * rows * 4 * rowlen + q * rows * rowlen,
                        [[rowlen, rows], [1, rowlen]])), inc="d_tp")
        for e in range(ELOC):
            for q in range(4):
                _emit_pc(wgb_d, wg_d, e, q, D // 4, FI)
                _emit_pc(wub_d, wu_d, e, q, D // 4, FI)
                _emit_pc(wdb_d, wd_d, e, q, FI // 4, D)
        pc_all = ("d_tp", p.cnt["d_tp"])

        def u_xq(tq):
            def emit():
                prev_users = list(st_ffn["xq_users"])
                st_ffn["xq_users"] = []
                prev_sxn = None
                for blk in range(4):
                    tbq = tq * 4 + blk
                    for dh in range(2):
                        tok_ld = p.add("gpsimd",
                            lambda g, tbq=tbq, dh=dh: g.dma_start(
                                x0q_sb[:, :],
                                bass.AP(x0_d, tbq * 128 * D + dh * 1024,
                                        [[D, 128], [1, 1024]])),
                            waits=[st_ffn["xq_ready"], prev_sxn],
                            inc="d_x0q")
                        sxn = p.add("scalar", lambda s, tbq=tbq: s.activation(
                            xnq_sb[:, :], x0q_sb[:, :], AF.Identity,
                            scale=stat0r_sb[:, tbq:tbq + 1],
                            bias=stat0n_sb[:, tbq:tbq + 1]),
                            waits=[tok_ld], inc="ss")
                        prev_sxn = sxn
                        for dq in range(2):
                            mm = None
                            for k in range(4):
                                dblk8 = dq * 4 + k
                                mm = p.add("tensor",
                                    lambda t, dblk8=dblk8, k=k: t.matmul(
                                        ps_t[:, k * 128:(k + 1) * 128],
                                        xnq_sb[:, dblk8 * 128:(dblk8 + 1) * 128],
                                        eye_sb[:, :], start=True, stop=True),
                                    waits=([sxn, st_ffn["pst_rd"]] + prev_users)
                                    if k == 0 else [],
                                    inc="st")
                            prev_users = []
                            st_ffn["pst_rd"] = p.add("scalar",
                                lambda s, dq=dq, dh=dh, blk=blk: s.activation(
                                    bass.AP(xq_sb,
                                            (dh * 8 + dq * 4) * TQ + blk * 128,
                                            [[G * TQ, 128], [TQ, 4], [1, 128]]),
                                    ps_t[:, :], AF.Copy),
                                waits=[mm], inc="ss")
                st_ffn["xq_ready"] = st_ffn["pst_rd"]
            return emit

        def u_gateup(tq, e, ib):
            def emit():
                wg_tok = p.add("sync", lambda s: s.dma_start(
                    wgs_sb[:, :, :],
                    bass.AP(wgb_d, e * D * FI + ib * 128,
                            [[FI, 128], [128 * FI, G], [1, 128]])),
                    waits=[st_ffn["wg_tok"], pc_all], inc="d_wg")
                wu_tok = p.add("sync", lambda s: s.dma_start(
                    wus_sb[:, :, :],
                    bass.AP(wub_d, e * D * FI + ib * 128,
                            [[FI, 128], [128 * FI, G], [1, 128]])),
                    waits=[st_ffn["wu_tok"], pc_all], inc="d_wu")
                mmg = mmu = None
                for hb in range(G):
                    mmg = p.add("tensor", lambda t, hb=hb: t.matmul(
                        ps_g[:, :], wgs_sb[:, hb, :],
                        bass.AP(xq_sb, hb * TQ, [[G * TQ, 128], [1, TQ]]),
                        start=(hb == 0), stop=(hb == G - 1)),
                        waits=[wg_tok, st_ffn["xq_ready"], st_ffn["psg_rd"]]
                        if hb == 0 else [],
                        inc="st")
                for hb in range(G):
                    mmu = p.add("tensor", lambda t, hb=hb: t.matmul(
                        ps_u[:, :], wus_sb[:, hb, :],
                        bass.AP(xq_sb, hb * TQ, [[G * TQ, 128], [1, TQ]]),
                        start=(hb == 0), stop=(hb == G - 1)),
                        waits=[wu_tok, st_ffn["psu_rd"]] if hb == 0 else [],
                        inc="st")
                st_ffn["wg_tok"] = mmg
                st_ffn["wu_tok"] = mmu
                st_ffn["xq_users"] = [mmu]
                s1 = p.add("scalar", lambda s: s.activation(
                    seg_sb[:, :], ps_g[:, :], AF.Exp, scale=-1.0),
                    waits=[mmg, st_ffn["seg_guard"]], inc="ss")
                s2 = p.add("scalar", lambda s: s.activation(
                    sil_sb[:, :], seg_sb[:, :], AF.Ln, bias=1.0),
                    waits=[s1, st_ffn["sil_guard"]], inc="ss")
                s3 = p.add("scalar", lambda s: s.activation(
                    seg_sb[:, :], sil_sb[:, :], AF.Exp, scale=-1.0),
                    waits=[s2], inc="ss")
                v1 = p.add("vector", lambda v: v.tensor_tensor(
                    sil_sb[:, :], seg_sb[:, :], ps_g[:, :], ALU.mult),
                    waits=[s3], inc="sv")
                st_ffn["seg_guard"] = v1
                st_ffn["psg_rd"] = v1
                v2 = p.add("vector", lambda v: v.tensor_tensor(
                    sil_sb[:, :], sil_sb[:, :], ps_u[:, :], ALU.mult),
                    waits=[v1, mmu], inc="sv")
                st_ffn["psu_rd"] = v2
                v3 = p.add("vector", lambda v, e=e, tq=tq, ib=ib:
                    v.tensor_tensor(
                        bass.AP(actq_sb, ib * TQ, [[IBLK * TQ, 128], [1, TQ]]),
                        sil_sb[:, :],
                        bass.AP(rrep_sb, e * L + tq * TQ,
                                [[ELOC * L, 128], [1, TQ]]),
                        ALU.mult),
                    waits=[v2, rrep_done] + st_ffn["act_users"], inc="sv")
                st_ffn["sil_guard"] = v3
                st_ffn["act_users"] = []
                st_ffn["act_toks"][ib] = v3
            return emit

        def u_down(tq, e, hb):
            def emit():
                wd_tok = p.add("sync", lambda s: s.dma_start(
                    wds_sb[:, :, :],
                    bass.AP(wdb_d, e * FI * D + hb * 128,
                            [[D, 128], [128 * D, IBLK], [1, 128]])),
                    waits=[st_ffn["wd_tok"], pc_all], inc="d_wd")
                mmd = None
                for ib in range(IBLK):
                    mmd = p.add("tensor", lambda t, ib=ib: t.matmul(
                        ps_d[:, :], wds_sb[:, ib, :],
                        bass.AP(actq_sb, ib * TQ, [[IBLK * TQ, 128], [1, TQ]]),
                        start=(ib == 0), stop=(ib == IBLK - 1)),
                        waits=([wd_tok, st_ffn["psd_rd"]] +
                               [t for t in st_ffn["act_toks"] if t])
                        if ib == 0 else [],
                        inc="st")
                st_ffn["wd_tok"] = mmd
                if hb == G - 1:
                    st_ffn["act_users"] = [mmd]
                par = st_ffn["fst_par"]
                st_ffn["fst_par"] = 1 - par
                ev = p.add("scalar", lambda s, par=par: s.activation(
                    fstage[par][:, :], ps_d[:, :], AF.Copy),
                    waits=[mmd, st_ffn["fst_guard"][par]], inc="ss")
                st_ffn["psd_rd"] = ev
                if e == 0:
                    tok = p.add("sync", lambda s, par=par, hb=hb, tq=tq:
                        s.dma_start(
                            bass.AP(ffn_in_d, hb * 128 * L + tq * TQ,
                                    [[L, 128], [1, TQ]]),
                            fstage[par][:, :]),
                        waits=[ev, st_ffn["fst_guard"][0]], inc="d_f")
                else:
                    tok = p.add("gpsimd", lambda g, par=par, hb=hb, tq=tq:
                        g.dma_start(
                            bass.AP(ffn_in_d, hb * 128 * L + tq * TQ,
                                    [[L, 128], [1, TQ]]),
                            fstage[par][:, :], accum_op=ALU.add),
                        waits=[ev, st_ffn["last_e0_dma"],
                               st_ffn["fst_guard"][0]], inc="d_f")
                st_ffn["last_dma"] = tok
                st_ffn["fst_guard"][par] = tok
            return emit

        ffn_units = []
        for tq in range(4) if not _DBG["idle_ffn"] else []:
            ffn_units.append(u_xq(tq))
            for e in range(ELOC):
                for ib in range(IBLK):
                    ffn_units.append(u_gateup(tq, e, ib))
                for hb in range(G):
                    ffn_units.append(u_down(tq, e, hb))
                if e == 0:
                    def mark_e0():
                        st_ffn["last_e0_dma"] = st_ffn["last_dma"]
                    ffn_units.append(mark_e0)

        def coll_unit():
            st_ffn["cc_tok"] = p.add("gpsimd", lambda g: g.collective_compute(
                "AllReduce", ALU.add,
                replica_groups=[list(range(NCORE))],
                ins=[ffn_in_d.ap()], outs=[ffn_out_d.ap()]),
                waits=[st_ffn["last_dma"]], inc="cc", amt=1)
        if not _DBG["idle_ffn"]:
            ffn_units.append(coll_unit)

        bc_guard = None
        psy_rd = None
        sp2_war = None
        psab_guard = [psab_guard[0], psab_guard[1]]
        a_guard = [None, None]
        b_guard = [None, None]
        dgc_guard = [None, None]
        yst_guard = [None, None, None]
        ych = ["d_y", "d_y1", "d_y2", "d_y3"]
        ylast = [None, None, None, None]
        ybody = [0]
        wave_tok = None
        ffn_i = 0

        def emit_A(cq, nb, g_i):
            nonlocal sp2_war
            par = g_i % 2
            mm = p.add("tensor", lambda t, g_i=g_i, cq=cq, par=par:
                t.matmul(PSAB[par][:, :], wdtT_sb[:, g_i, :],
                         dtr_sb[:, cq * TQ:(cq + 1) * TQ],
                         start=True, stop=True),
                waits=[psab_guard[par]], inc="st")
            s1 = p.add("scalar", lambda s, g_i=g_i, par=par:
                s.activation(sp2_sb[:, :], PSAB[par][:, :], AF.Exp,
                             bias=bdt_sb[:, g_i:g_i + 1]),
                waits=[mm, sp2_war], inc="ss")
            psab_guard[par] = s1
            s2 = p.add("scalar", lambda s, par=par: s.activation(
                dgc_sb[par][:, :], sp2_sb[:, :], AF.Ln, bias=1.0),
                waits=[s1, dgc_guard[par]], inc="ss")
            sp2_war = s2
            atoks = []
            for n in range(NBS):
                ng = nb * NBS + n
                atoks.append(p.add("scalar",
                    lambda s, n=n, ng=ng, g_i=g_i, par=par:
                    s.activation(
                        bass.AP(a8[par], n * TQ,
                                [[NBS * TQ, 128], [1, TQ]]),
                        dgc_sb[par][:, :], AF.Exp,
                        scale=A_sb[:, g_i, ng:ng + 1]),
                    waits=[s2, t_a, a_guard[par]] if n == 0
                    else [], inc="ss"))
            dgc_guard[par] = atoks[-1]
            return atoks[-1]

        def emit_B(cq, nb, g_i, a_ready, bc_ready):
            nonlocal bc_guard, psy_rd
            par = g_i % 2
            v1 = p.add("vector", lambda v, par=par, g_i=g_i, nb=nb:
                v.tensor_tensor(
                    cfix_sb[:, :],
                    bass.AP(a8[par], 0, [[NBS * TQ, 128], [TQ, NBS]]),
                    bass.AP(hcol_sb, g_i * NST + nb * NBS,
                            [[G * NST, 128], [1, NBS]]),
                    ALU.mult),
                waits=[a_ready, t_hc0], inc="sv")
            v2 = p.add("vector", lambda v, par=par: v.memset(
                bass.AP(a8[par], 0,
                        [[NBS * TQ, 128], [TQ, NBS], [1, 1]]),
                0.0), waits=[v1], inc="sv")
            v3 = p.add("vector", lambda v, par=par, g_i=g_i, cq=cq:
                v.tensor_tensor(
                    b8[par][:, :, :],
                    bass.AP(du_sb, g_i * L + cq * TQ,
                            [[G * L, 128], [0, NBS], [1, TQ]]),
                    B8_sb[:, :, :], ALU.mult),
                waits=[bc_ready, du_done, v2, b_guard[par]],
                inc="sv")
            v4 = p.add("vector", lambda v, par=par: v.tensor_tensor(
                cfix2_sb[:, :], cfix_sb[:, :],
                bass.AP(b8[par], 0, [[NBS * TQ, 128], [TQ, NBS]]),
                ALU.add), waits=[v3], inc="sv")
            v4b = p.add("vector", lambda v, par=par: v.tensor_copy(
                bass.AP(b8[par], 0, [[NBS * TQ, 128], [TQ, NBS]]),
                cfix2_sb[:, :]), waits=[v4], inc="sv")
            v5 = p.add("vector", lambda v, par=par: v.tensor_tensor_scan(
                bass.AP(H8_sb, 0, [[NBS * TQ, 128], [1, NBS * TQ]]),
                bass.AP(a8[par], 0, [[NBS * TQ, 128], [1, NBS * TQ]]),
                bass.AP(b8[par], 0, [[NBS * TQ, 128], [1, NBS * TQ]]),
                0.0, ALU.mult, ALU.add), waits=[v4b], inc="sv")
            a_guard[par] = v5
            v6 = p.add("vector", lambda v, g_i=g_i, nb=nb:
                v.tensor_copy(
                    bass.AP(hcol_sb, g_i * NST + nb * NBS,
                            [[G * NST, 128], [1, NBS]]),
                    bass.AP(H8_sb, TQ - 1, [[NBS * TQ, 128], [TQ, NBS]])),
                waits=[v5], inc="sv")
            v7 = p.add("vector", lambda v, par=par: v.tensor_tensor(
                b8[par][:, :, :], H8_sb[:, :, :], C8_sb[:, :, :],
                ALU.mult), waits=[v6], inc="sv")
            bc_guard = v7
            mmy = None
            for n in range(NBS):
                for j in range(4):
                    mmy = p.add("tensor",
                        lambda t, par=par, n=n, j=j: t.matmul(
                            ps_y[:, j * 128:(j + 1) * 128],
                            bass.AP(b8[par], n * TQ + j * 128,
                                    [[NBS * TQ, 128], [1, 128]]),
                            eye_sb[:, :],
                            start=(n == 0 and j == 0),
                            stop=(n == NBS - 1 and j == 3)),
                        waits=[v7, psy_rd] if n == 0 and j == 0
                        else [], inc="st")
            b_guard[par] = mmy
            ypar = ybody[0] % 3
            ev = p.add("scalar", lambda s, ypar=ypar: s.activation(
                ystage[ypar][:, :], ps_y[:, :], AF.Copy),
                waits=[mmy, yst_guard[ypar]], inc="ss")
            psy_rd = ev
            ci = ybody[0] % 4
            ybody[0] += 1
            ydma = p.add("gpsimd", lambda g, ypar=ypar, cq=cq, g_i=g_i:
                g.dma_start(
                    bass.AP(y_out_d, (cq * TQ) * D + g_i * 128,
                            [[D, 128], [128 * D, 4], [1, 128]]),
                    bass.AP(ystage[ypar], 0,
                            [[TQ, 128], [128, 4], [1, 128]]),
                    accum_op=ALU.add),
                waits=[ev, out_pre_done, ylast[ci]], inc=ych[ci])
            ylast[ci] = ydma
            yst_guard[ypar] = ydma

        for cq in range(CQ):
            for nb in range(NBQ):
                bc_toks = []
                for n in range(NBS):
                    ng = nb * NBS + n
                    par = n % 2
                    mmB = p.add("tensor", lambda t, ng=ng, par=par, cq=cq:
                        t.matmul(
                        PSAB[par][:, :],
                        bass.AP(eye_sb, ng, [[128, NST], [0, 128]]),
                        B_sb[:, cq * TQ:(cq + 1) * TQ],
                        start=True, stop=True),
                        waits=[t_B, t_sel, bc_guard, psab_guard[par]],
                        inc="st")
                    evB = p.add("scalar", lambda s, n=n, par=par: s.activation(
                        bass.AP(B8_sb, n * TQ, [[NBS * TQ, 128], [1, TQ]]),
                        PSAB[par][:, :], AF.Copy), waits=[mmB], inc="ss")
                    mmC = p.add("tensor", lambda t, ng=ng, par=par, cq=cq:
                        t.matmul(
                        PSAB[par][:, :],
                        bass.AP(eye_sb, ng, [[128, NST], [0, 128]]),
                        C_sb[:, cq * TQ:(cq + 1) * TQ],
                        start=True, stop=True),
                        waits=[t_C, evB], inc="st")
                    evC = p.add("scalar", lambda s, n=n, par=par: s.activation(
                        bass.AP(C8_sb, n * TQ, [[NBS * TQ, 128], [1, TQ]]),
                        PSAB[par][:, :], AF.Copy), waits=[mmC], inc="ss")
                    psab_guard[par] = evC
                    bc_toks.append(evC)
                bc_ready = bc_toks[-1]

                pend = None
                for g_i in range(G):
                    a_ready = emit_A(cq, nb, g_i)
                    if pend is not None:
                        emit_B(*pend)
                        if ffn_i < len(ffn_units):
                            ffn_units[ffn_i]()
                            ffn_i += 1
                    pend = (cq, nb, g_i, a_ready, bc_ready)
                emit_B(*pend)
                if ffn_i < len(ffn_units):
                    ffn_units[ffn_i]()
                    ffn_i += 1
                wave_tok = None
        while ffn_i < len(ffn_units):
            ffn_units[ffn_i]()
            ffn_i += 1


        tst_ap = bass.AP(sil_sb, 0, [[TQ, 128], [128, 4], [1, 128]])
        fin_iter = range(G) if not _DBG["idle_ffn"] else range(0)
        fin_ap = bass.AP(a8[0], 0, [[NBS * TQ, 128], [1, L]])
        last_fin_dma = None
        for hb in fin_iter:
            tok_ld = p.add("gpsimd", lambda g, hb=hb: g.dma_start(
                fin_ap,
                bass.AP(ffn_out_d, hb * 128 * L, [[L, 128], [1, L]])),
                waits=[st_ffn["cc_tok"], last_fin_dma, a_guard[0],
                       b_guard[0], bc_guard], inc="d_fin")
            for dq in range(4):
                mm = None
                for k in range(4):
                    tblk = dq * 4 + k
                    mm = p.add("tensor", lambda t, tblk=tblk, k=k: t.matmul(
                        ps_t[:, k * 128:(k + 1) * 128],
                        bass.AP(a8[0], tblk * 128, [[NBS * TQ, 128], [1, 128]]),
                        eye_sb[:, :], start=True, stop=True),
                        waits=[tok_ld, st_ffn["pst_rd"]] if k == 0 else [],
                        inc="st")
                ev = p.add("scalar", lambda s, dq=dq: s.activation(
                    tst_ap, ps_t[:, :], AF.Identity,
                    scale=m0_sb[:, 0:1]),
                    waits=[mm, last_fin_dma, st_ffn["sil_guard"]], inc="ss")
                st_ffn["pst_rd"] = ev
                last_fin_dma = p.add("gpsimd", lambda g, hb=hb, dq=dq:
                    g.dma_start(
                        bass.AP(f_out_d, (dq * 4 * 128) * D + hb * 128,
                                [[D, 128], [128 * D, 4], [1, 128]]),
                        tst_ap, accum_op=ALU.add),
                    waits=[ev, last_fin_dma] + [(c_, p.cnt[c_])
                           for c_ in ("d_y", "d_y1", "d_y2", "d_y3")],
                    inc="d_fin")

        p.add("sync", None,
              waits=[(c, v) for c, v in p.cnt.items() if v > 0])

        with nc.Block() as block:
            @block.sync
            def _(e):
                p.play("sync", e, sems)

            @block.gpsimd
            def _(e):
                p.play("gpsimd", e, sems)

            @block.vector
            def _(e):
                p.play("vector", e, sems)

            @block.scalar
            def _(e):
                p.play("scalar", e, sems)

            @block.tensor
            def _(e):
                p.play("tensor", e, sems)

    return nc


def kernel(L0_attn_x, L0_attn_z, L0_attn_W_bc, L0_attn_W_dt, L0_attn_b_dt,
           L0_attn_A_log, L0_attn_D_param, L0_ffn_topk_idx, L0_ffn_topk_weight,
           L0_ffn_gate_weight, L0_ffn_up_weight, L0_ffn_down_weight):
    x = np.ascontiguousarray(np.asarray(L0_attn_x, dtype=np.float32))
    wbcT = np.ascontiguousarray(np.asarray(L0_attn_W_bc, np.float32).T)
    wdtT = np.ascontiguousarray(np.asarray(L0_attn_W_dt, np.float32).T)
    bdt = np.ascontiguousarray(np.asarray(L0_attn_b_dt, np.float32))
    alog = np.ascontiguousarray(np.asarray(L0_attn_A_log, np.float32))
    dp = np.ascontiguousarray(np.asarray(L0_attn_D_param, np.float32))
    tki = np.ascontiguousarray(np.asarray(L0_ffn_topk_idx, np.int32).reshape(L, 4))
    tkw = np.ascontiguousarray(
        np.asarray(L0_ffn_topk_weight, np.float32).reshape(L, 4))
    wgT = np.ascontiguousarray(
        np.asarray(L0_ffn_gate_weight, np.float32).transpose(0, 2, 1))
    wuT = np.ascontiguousarray(
        np.asarray(L0_ffn_up_weight, np.float32).transpose(0, 2, 1))
    wdT = np.ascontiguousarray(
        np.asarray(L0_ffn_down_weight, np.float32).transpose(0, 2, 1))

    nc = build()
    in_maps = []
    for c in range(NCORE):
        in_maps.append(dict(
            xb=np.ascontiguousarray(x[c]),
            x0=np.ascontiguousarray(x[0]),
            wbcT=wbcT, wdtT=wdtT, bdt=bdt, alog=alog, dp=dp,
            tki=tki, tkw=tkw,
            wg=np.ascontiguousarray(wgT[2 * c:2 * c + 2]),
            wu=np.ascontiguousarray(wuT[2 * c:2 * c + 2]),
            wd=np.ascontiguousarray(wdT[2 * c:2 * c + 2]),
            ecol=np.tile(np.array([[2 * c, 2 * c + 1]], np.float32), (128, 1)),
            m0=np.full((128, 1), 1.0 if c == 0 else 0.0, np.float32),
        ))
    res = run_bass_kernel_spmd(nc, in_maps, core_ids=list(range(NCORE)))
    return np.stack([np.asarray(res.results[c]["out"], np.float32)
                     for c in range(NCORE)], axis=0)
